# version 31
# baseline (speedup 1.0000x reference)
import numpy as np
import concourse.bass as bass
import concourse.mybir as mybir
from concourse.bass_utils import run_bass_kernel_spmd

F32 = mybir.dt.float32
BF16 = mybir.dt.bfloat16
AF = mybir.ActivationFunctionType
ALU = mybir.AluOpType
AX = mybir.AxisListType
AP = bass.AP

STAGE = "full"

T = 4096
NCH = 8
CST = 4104
NSTRIP = 8
SW = 512
HO, DO = 16, 64
HL, DL, WIN = 8, 64, 512
C_OUTER = 16.0
C_LOCAL = 8.0
HID = 2730
HIDP = 2816
NC2 = 22
NC1 = 44

SBASE = 16640
OFF_A = SBASE + 0
OFF_STREAM = SBASE + 65664
TAIL = SBASE + 197888
TAIL2 = TAIL + 10240
TAIL3 = TAIL2 + 2048
TAIL4 = TAIL2 + 4096

ENGINES = ("pe", "act", "dve", "pool", "sp")


class Sem:
    def __init__(self, handle, producer, name, is_dma=False):
        self.h, self.producer, self.name, self.count = handle, producer, name, 0
        self.is_dma = is_dma


class Builder:
    def __init__(self, nc):
        self.nc = nc
        self.steps = {e: [] for e in ENGINES}
        self.sems = []
        self.dma_sems = []
        self.waited = {e: {} for e in ENGINES}
        self.nwaits = 0

    def sem(self, name, producer, is_dma=False):
        h = self.nc.semaphore(name)
        s = Sem(h.__enter__(), producer, name, is_dma)
        self.sems.append((s, h))
        if is_dma:
            self.dma_sems.append(s)
        return s

    def emit(self, eng, fn, waits=(), inc=None, inc_amt=None, tag=""):
        w = []
        for ev in waits:
            if ev is None:
                continue
            s, v = ev
            if s.producer == eng and not s.is_dma:
                continue
            if v <= self.waited[eng].get(s.name, -1):
                continue
            self.waited[eng][s.name] = v
            w.append((s, v))
        self.nwaits += len(w)
        out_ev = None
        amt = 0
        if inc is not None:
            amt = inc_amt if inc_amt is not None else (16 if inc.producer in ("sp", "pool") else 1)
            inc.count += amt
            out_ev = (inc, inc.count)
        self.steps[eng].append((w, fn, inc, amt, tag))
        return out_ev

    def dma(self, q, sem, out_ap, in_ap, waits=(), tag="", **kw):
        return self.emit(q, (lambda o, i, kw2: (lambda e: e.dma_start(o, i, **kw2)))(out_ap, in_ap, kw),
                         waits=waits, inc=sem, tag=tag)

    def wait_only(self, eng, waits, tag=""):
        self.emit(eng, None, waits=waits, tag=tag)

    def barrier(self, sems_by_eng, tag="bar"):
        for e in ("sp", "pool"):
            self.wait_only(e, [(sm, sm.count) for sm in self.dma_sems
                              if sm.producer == e and sm.count], tag=tag + "_q")
        evs = []
        for e in ENGINES:
            evs.append(self.emit(e, lambda en: en.drain(), inc=sems_by_eng[e], inc_amt=1, tag=tag))
        for e in ENGINES:
            self.wait_only(e, evs, tag=tag + "_w")
        return evs

    def replay(self, block):
        def body(engname):
            def run(engobj):
                for (w, fn, inc, amt, tag) in self.steps[engname]:
                    for (s, v) in w:
                        engobj.wait_ge(s.h, v)
                    if fn is not None:
                        inst = fn(engobj)
                        if inc is not None:
                            inst.then_inc(inc.h, amt)
            return run
        block.tensor(body("pe"))
        block.scalar(body("act"))
        block.vector(body("dve"))
        block.gpsimd(body("pool"))
        block.sync(body("sp"))

    def close(self):
        for (_, h) in self.sems:
            h.__exit__(None, None, None)


class Ring:
    def __init__(self):
        self.d = {}

    def get(self, i):
        return self.d.get(i)

    def set(self, i, ev):
        self.d[i] = ev


class Consts:
    def __init__(self, t):
        self.t = t
        self.vals = {}

    def col(self, v, p0=0, np_=128):
        i = self.vals[float(v)]
        return self.t[p0:p0 + np_, i:i + 1]

    def reg(self, *vs):
        for v in vs:
            v = float(v)
            if v not in self.vals:
                self.vals[v] = len(self.vals)
        assert len(self.vals) <= self.t.shape[1]

    def emit_init(self, bld, inc):
        ev = None
        items = list(self.vals.items())
        for j, (v, i) in enumerate(items):
            ev = bld.emit("dve", (lambda vv, ii: (lambda e: e.memset(self.t[:, ii:ii + 1], vv)))(v, i),
                          inc=inc if j == len(items) - 1 else None, tag="const")
        return ev


def _host_prep(z):
    f32c = lambda a: np.ascontiguousarray(np.asarray(a, dtype=np.float32))
    sh = {}
    x = f32c(z["L0_attn_x"])
    xs = [np.ascontiguousarray(x[i].T) for i in range(8)]

    sh["wv_t"] = f32c(z["L0_attn_wv_w"].T)
    sh["wv_b"] = f32c(z["L0_attn_wv_b"]).reshape(1, 1024)
    w = f32c(z["L0_attn_w_att_val"])
    sh["watt"] = np.ascontiguousarray(
        np.concatenate([w, w], axis=2).transpose(1, 0, 2).reshape(64, 16 * 128))
    sh["outw_t"] = f32c(z["L0_attn_out_w"].T)
    sh["outb_col"] = np.ascontiguousarray(f32c(z["L0_attn_out_b"]).reshape(8, 128).T)

    sh["wqkv"] = f32c(z["L0_ffn_wqkv"])
    sh["wo"] = f32c(z["L0_ffn_wo"])
    sh["qs_t"] = np.ascontiguousarray(np.broadcast_to(f32c(z["L0_ffn_q_scale"])[:, None, :], (2, 128, 64)))
    sh["ks_t"] = np.ascontiguousarray(np.broadcast_to(f32c(z["L0_ffn_k_scale"])[:, None, :], (2, 128, 64)))
    sh["lng_col"] = np.ascontiguousarray(f32c(z["L0_ffn_ln_g"]).reshape(2, 8, 128).transpose(0, 2, 1))
    sh["lnb_col"] = np.ascontiguousarray(f32c(z["L0_ffn_ln_b"]).reshape(2, 8, 128).transpose(0, 2, 1))

    w1 = f32c(z["L0_ffn_w1"])
    b1 = f32c(z["L0_ffn_b1"])
    w1p = np.zeros((2, 1024, 2 * HIDP), np.float32)
    b1a = np.zeros((2, HIDP), np.float32)
    b1g = np.zeros((2, HIDP), np.float32)
    for c in range(NC2):
        lo, hi = c * 128, min((c + 1) * 128, HID)
        n = hi - lo
        w1p[:, :, (2 * c) * 128:(2 * c) * 128 + n] = w1[:, :, lo:hi]
        w1p[:, :, (2 * c + 1) * 128:(2 * c + 1) * 128 + n] = w1[:, :, HID + lo:HID + hi]
        b1a[:, c * 128:c * 128 + n] = b1[:, lo:hi]
        b1g[:, c * 128:c * 128 + n] = b1[:, HID + lo:HID + hi]
    sh["w1t"] = np.ascontiguousarray(
        w1p.reshape(2, 8, 128, NC1, 128).transpose(0, 3, 1, 2, 4))
    sh["b1a_col"] = np.ascontiguousarray(b1a.reshape(2, NC2, 128).transpose(0, 2, 1))
    sh["b1g_col"] = np.ascontiguousarray(b1g.reshape(2, NC2, 128).transpose(0, 2, 1))

    w2 = f32c(z["L0_ffn_w2"])
    w2p = np.zeros((2, HIDP, 1024), np.float32)
    w2p[:, :HID, :] = w2
    sh["w2t"] = np.ascontiguousarray(
        w2p.reshape(2, NC2, 128, 8, 128).transpose(0, 3, 1, 2, 4))
    sh["b2_col"] = np.ascontiguousarray(f32c(z["L0_ffn_b2"]).reshape(2, 8, 128).transpose(0, 2, 1))

    half = np.arange(0, DL, 2, dtype=np.float32)
    inv = (1.0 / 10000.0 ** (half / DL)).astype(np.float32)
    t = np.arange(T, dtype=np.float32)
    fr = (t[:, None] * inv[None, :]).astype(np.float32)
    fr = np.concatenate([fr, fr], 1)
    base = ((half + 0.4 * DL) / (1.4 * DL)).astype(np.float32)
    power = ((t - T // 2) / float(WIN // 2)).astype(np.float32)
    sc = (base[None, :] ** power[:, None]).astype(np.float32)
    sc = np.concatenate([sc, sc], 1)
    cos = np.cos(fr).astype(np.float32)
    sin = np.sin(fr).astype(np.float32)
    sgn = np.concatenate([-np.ones(32, np.float32), np.ones(32, np.float32)])
    sh["rot_csq"] = np.ascontiguousarray(cos * sc)
    sh["rot_ssq"] = np.ascontiguousarray(sin * sc * sgn)
    sh["rot_csk"] = np.ascontiguousarray(cos / sc)
    sh["rot_ssk"] = np.ascontiguousarray(sin / sc * sgn)

    tri = np.triu(np.ones((128, 128), np.float32))
    sh["masks"] = np.ascontiguousarray(np.stack(
        [-50.0 * (1.0 - tri.T).T, -50.0 * (1.0 - tri).T], 0))
    sh["ident"] = np.eye(128, dtype=np.float32)
    return sh, xs


def build_program(stage):
    nc = bass.Bass("TRN2", target_bir_lowering=False, detect_race_conditions=False)

    D = {}
    def din(name, shape):
        D[name] = nc.dram_tensor(name, list(shape), F32, kind="ExternalInput")
    din("x_fm", [1024, T])
    din("wv_t", [1024, 1024]); din("wv_b", [1, 1024])
    din("watt", [64, 16 * 128]); din("outw_t", [1024, 1024]); din("outb_col", [128, 8])
    din("wqkv", [2, 1024, 1536]); din("wo", [2, 512, 1024])
    din("qs_t", [2, 128, 64]); din("ks_t", [2, 128, 64])
    din("lng_col", [2, 128, 8]); din("lnb_col", [2, 128, 8])
    din("w1t", [2, NC1, 8, 128, 128]); din("b1a_col", [2, 128, NC2]); din("b1g_col", [2, 128, NC2])
    din("w2t", [2, 8, NC2, 128, 128]); din("b2_col", [2, 128, 8])
    din("rot_csq", [T, 64]); din("rot_ssq", [T, 64])
    din("rot_csk", [T, 64]); din("rot_ssk", [T, 64])
    din("masks", [2, 128, 128]); din("ident", [128, 128])
    x1d = nc.dram_tensor("x1d", [1025, T], F32, kind="Internal")
    OUTW_OFF, W1_OFF, W2_OFF = 0, 1024 * 1024, 1024 * 1024 + 2 * 44 * 8 * 128 * 128
    NW_SCR = W2_OFF + 2 * 8 * 22 * 128 * 128
    wscr = nc.dram_tensor("wscr", [NW_SCR], BF16, kind="Internal")
    out_fm = nc.dram_tensor("out_fm", [1024, T], F32, kind="ExternalOutput")

    import os as _osA
    _MIN = _osA.environ.get("K_MINALLOC", "0") == "1"
    _min_skip = {"stream", "o_buf", "outw_buf", "x0_buf", "s1bf", "sqbf", "rowb", "ident_sb"}
    def at(name, shape, dt, off):
        if _MIN and name in _min_skip:
            return None
        return nc.alloc_sbuf_tensor_at(name, shape, dt, offset=off)

    o_buf = at("o_buf", [128, 8 * 4096], BF16, OFF_A)
    wv_sb = at("wv_sb", [128, 8 * 1024], BF16, OFF_A)
    wvb32 = at("wvb32", [1, 1024], F32, OFF_A + 16384)
    wvbh32 = at("wvbh32", [1, 1024], F32, OFF_A + 20480)
    wvb2 = at("wvb2", [1, 2048], BF16, OFF_A + 24576)
    x_bf = at("x_bf", [128, 8 * CST], BF16, OFF_STREAM)
    vtm = at("vtm", [128, 32 * 1040], BF16, OFF_STREAM + 65664)
    stream = at("stream", [128, 8 * CST], F32, OFF_STREAM)

    watt_sb = at("watt_sb", [128, 16 * 128], BF16, TAIL + 0)
    xw_sb = at("xw_sb", [128, 2 * 512], BF16, TAIL + 4096)
    pq_buf = at("pq_buf", [128, 4 * 512], BF16, TAIL + 6144)
    outw_buf = at("outw_buf", [128, 2 * 8 * 128], BF16, TAIL + 0)
    x0_buf = at("x0_buf", [128, 512], F32, TAIL + 4096)
    s1bf = at("s1bf", [128, 2 * 512], BF16, TAIL + 6144)
    sqbf = at("sqbf", [128, 2 * 512], BF16, TAIL + 8192)
    wqkv_sb = at("wqkv_sb", [128, 8 * 1536], BF16, OFF_A + 0)
    wo_sb = at("wo_sb", [128, 4 * 1024], BF16, OFF_A + 24576)
    kfm = at("kfm", [128, 4 * 2 * 512], BF16, OFF_A + 32768)
    vloc = at("vloc", [128, 2 * 4 * 520], BF16, OFF_A + 40960)
    qfm = at("qfm", [128, 4 * 512], BF16, OFF_A + 49280)
    oloc = at("oloc", [128, 4 * 512], BF16, OFF_A + 53376)
    rotb = at("rotb", [128, 2 * 4 * 64], F32, OFF_A + 57472)
    ploc = at("ploc", [128, 4 * 512], BF16, OFF_A + 59520)
    lcols = at("lcols", [128, 80], F32, OFF_A + 63616)
    l2buf = at("l2buf", [128, 32], F32, OFF_A + 63936)
    bound = at("bound", [128, 64], F32, OFF_A + 64064)
    masks_sb = at("masks_sb", [128, 256], BF16, OFF_A + 64320)
    qsks = at("qsks", [128, 128], F32, OFF_A + 64832)
    act_st = at("act_st", [128, 22 * 512], BF16, OFF_A + 0)
    h2bf = at("h2bf", [128, 8 * 512], BF16, OFF_A + 22528)
    w1buf = at("w1buf", [128, 2 * 16 * 128], BF16, OFF_A + 30720)
    w2buf = at("w2buf", [128, 2 * 22 * 128], BF16, OFF_A + 38912)
    a_st = at("a_st", [128, 512], F32, OFF_A + 50176)
    g_st = at("g_st", [128, 512], F32, OFF_A + 52224)
    q32 = at("q32", [128, 512], F32, TAIL + 0)
    qfm_z = at("qfm_z", [128, 1024], BF16, TAIL + 0)
    qt1 = at("qt1", [128, 512], F32, TAIL + 2048)
    qt2 = at("qt2", [128, 512], F32, TAIL + 4096)
    qbf = at("qbf", [128, 512], BF16, TAIL + 6144)
    hbuf = at("hbuf", [128, 8 * 128], BF16, TAIL + 7168)
    x1b = at("x1b", [128, 512], F32, TAIL + 7168)
    sqb2 = at("sqb2", [128, 2 * 512], BF16, TAIL + 0)
    rowf = at("rowf", [128, 1024], F32, TAIL2)
    rowb = at("rowb", [128, 1024], BF16, TAIL3)
    cpool = at("cpool", [128, 16], F32, TAIL4)
    outb_sb = at("outb_sb", [128, 8], F32, TAIL4 + 64)
    ones_bf = at("ones_bf", [128, 128], BF16, TAIL4 + 96)
    ident_sb = at("ident_sb", [128, 128], BF16, OFF_A + 65344)

    pg = []
    def ps(name, shape, dt=F32):
        g = nc.psum_tensor(name, shape, dt)
        tt = g.__enter__(); pg.append(g); return tt
    ps_a = ps("ps_a", [128, 2048])
    ps_b = ps("ps_b", [128, 512])
    ps_c = ps("ps_c", [128, 512])
    ps_d = ps("ps_d", [128, 512])
    pt_t = ps("pt_t", [128, 1024], BF16)

    bld = Builder(nc)
    sP = bld.sem("sP", "pool", is_dma=True)
    sS = bld.sem("sS", "sp", is_dma=True)
    sE = bld.sem("sE", "pe")
    sA = bld.sem("sA", "act")
    sV = bld.sem("sV", "dve")
    sPb = bld.sem("sPb", "pool")
    sSb = bld.sem("sSb", "sp")
    sD0 = bld.sem("sD0", "sp", is_dma=True)
    sD1 = bld.sem("sD1", "sp", is_dma=True)
    sN1 = bld.sem("sN1", "sp", is_dma=True)
    sN2 = bld.sem("sN2", "sp", is_dma=True)
    sR = [bld.sem(f"sR{i}", "sp", is_dma=True) for i in range(8)]
    sX = [bld.sem(f"sX{i}", "pool", is_dma=True) for i in range(8)]
    sWV = bld.sem("sWV", "pool", is_dma=True)
    sWA = [bld.sem(f"sWA{i}", "pool", is_dma=True) for i in range(2)]
    sVB = bld.sem("sVB", "sp", is_dma=True)
    sOB = bld.sem("sOB", "sp", is_dma=True)
    sOW = [bld.sem(f"sOW{i}", "pool", is_dma=True) for i in range(2)]
    sX0 = bld.sem("sX0b", "sp", is_dma=True)
    sID = bld.sem("sID", "pool", is_dma=True)
    sMK = bld.sem("sMK", "pool", is_dma=True)
    sWQ = bld.sem("sWQ", "pool", is_dma=True)
    sWO = bld.sem("sWO", "pool", is_dma=True)
    sQS = bld.sem("sQS", "sp", is_dma=True)
    sKS = bld.sem("sKS", "sp", is_dma=True)
    sLC = [bld.sem(f"sLC{i}", "sp", is_dma=True) for i in range(6)]
    sW1a = [bld.sem(f"sW1a{i}", "pool", is_dma=True) for i in range(2)]
    sW1b = [bld.sem(f"sW1b{i}", "pool", is_dma=True) for i in range(2)]
    sW2a = [bld.sem(f"sW2a{i}", "pool", is_dma=True) for i in range(2)]
    sW2b = [bld.sem(f"sW2b{i}", "pool", is_dma=True) for i in range(2)]
    sPC = bld.sem("sPC", "pool", is_dma=True)
    SEMS = {"pe": sE, "act": sA, "dve": sV, "pool": sPb, "sp": sSb}
    bld.queue_sems = {"pool": sP, "sp": sS}

    import os as _os0
    _SKIP = set(_os0.environ.get("K_SKIP", "").split(","))
    CC = Consts(cpool)
    CC.reg(0.0, 1.0, -1.0, 0.125, -C_OUTER, -C_LOCAL, 1.0 / 1024.0, 1e-5, 8.0, 1e-24, -0.5,
           float(np.log(8.0)))
    if "consts" in _SKIP:
        ev_cc = bld.emit("dve", lambda e: e.memset(cpool[:], 0.0), inc=sV, tag="cz")
    else:
        ev_cc = CC.emit_init(bld, inc=sV)
    ev_ones = bld.emit("dve", lambda e: e.memset(ones_bf[:], 1.0), inc=sV, tag="ones")
    if "outb" in _SKIP:
        ev_outb = None
    else:
        ev_outb = bld.dma("sp", sOB, outb_sb[:], D["outb_col"].ap(), tag="outb")

    def finish(nc_):
        with nc_.Block() as blk:
            bld.replay(blk)
        bld.close()
        for g in pg[::-1]:
            g.__exit__(None, None, None)
        counts = {e: len(bld.steps[e]) for e in ENGINES}
        counts["waits"] = bld.nwaits
        print("[kernel] step counts:", counts)
        return nc_

    def finish_dump_bf16(src_ap):
        ev_o = bld.dma("pool", sP, AP(out_fm, 0, [[T, 128], [128 * T, 8], [1, 4096]]), src_ap, tag="dump")
        bld.wait_only("pool", [ev_o], tag="final")
        return finish(nc)

    def st_ap(c, t0, w):
        return stream[:, c * CST + 1 + t0: c * CST + 1 + t0 + w]

    def xbf_ap(c, t0, w, base=0, np_=128):
        return x_bf[base:base + np_, c * CST + 1 + t0: c * CST + 1 + t0 + w]

    ev_xbf = []
    _ULTRA = _os0.environ.get("K_ULTRA", "0") == "1"
    _U8 = _os0.environ.get("K_U8", "0") == "1"
    for c in range((8 if _U8 else 1) if _ULTRA else NCH):
        src = AP(D["x_fm"], c * 128 * T, [[T, 128], [1, T]])
        ev_xbf.append(bld.dma("pool", sX[c % 8], xbf_ap(c, 0, T), src, tag=f"xbf{c}"))
    if _ULTRA:
        _UADD = set(_os0.environ.get("K_UADD", "").split(","))
        ev_wvU = bld.dma("pool", sWV, wv_sb[:],
                         AP(D["wv_t"], 0, [[1024, 128], [128 * 1024, 8], [1, 1024]]), tag="wvU")
        if "watt" in _UADD:
            bld.dma("pool", sWA[0], watt_sb[0:64, :], D["watt"].ap(), tag="uwa0")
            bld.dma("pool", sWA[1], watt_sb[64:128, :], D["watt"].ap(), tag="uwa1")
        if "hilo" in _UADD:
            ev_wvbU = bld.dma("sp", sVB, wvb32[:], D["wv_b"].ap(), tag="uwvb")
            bld.emit("dve", lambda e: e.tensor_copy(wvb2[0:1, 0:1024], wvb32[:]), waits=[ev_wvbU], tag="uh1")
            bld.emit("dve", lambda e: e.tensor_copy(wvbh32[:], wvb2[0:1, 0:1024]), tag="uh2")
            bld.emit("dve", lambda e: e.tensor_sub(wvb2[0:1, 1024:2048], wvb32[:], wvbh32[:]), inc=sV, tag="uh3")
        if "vones" in _UADD:
            bld.emit("dve", lambda e: e.memset(AP(vtm, 64, [[32 * 1040, 128], [65, 512], [1, 1]]), 1.0),
                     inc=sV, tag="uvones")
        ev_m1 = bld.emit("pe", lambda e: e.matmul(ps_a[:, 0:512], x_bf[:, 1:129],
                                                  wv_sb[:, 0:512], start=True, stop=True),
                         waits=[ev_xbf[0], ev_wvU], inc=sE, tag="umm")
        ev_cp = bld.emit("dve", lambda e: e.tensor_copy(vtm[:, 0:512], ps_a[:, 0:512]),
                         waits=[ev_m1], inc=sV, tag="ucp")
        if "barq" in _UADD:
            for e in ("sp", "pool"):
                sm = bld.queue_sems[e]
                bld.wait_only(e, [(sm, sm.count)], tag="ubq")
        if "barn" in _UADD:
            for e in ENGINES:
                bld.emit(e, lambda en: en.nop(), inc=SEMS[e], inc_amt=1, tag="ubn")
        for e in ENGINES:
            if ("barn_" + e) in _UADD:
                bld.emit(e, lambda en: en.drain(), inc=SEMS[e], inc_amt=1, tag="ubn1")
        if "bar" in _UADD:
            bld.barrier(SEMS, tag="ubar")
        ev_o = bld.dma("pool", sP, AP(out_fm, 0, [[T, 128], [128 * T, 8], [1, 4096]]),
                       x_bf[:, 0:32768], waits=[ev_cp], tag="udump")
        bld.wait_only("pool", [ev_o], tag="final")
        return finish(nc)
    if stage == "Z0":
        ev_o = bld.dma("pool", sP, AP(out_fm, 0, [[T, 128], [128 * T, 8], [1, 4096]]),
                       x_bf[:, 0:32768], waits=ev_xbf, tag="dumpz")
        bld.wait_only("pool", [ev_o], tag="final")
        return finish(nc)
    ev_wv = bld.dma("pool", sWV, wv_sb[:],
                    AP(D["wv_t"], 0, [[1024, 128], [128 * 1024, 8], [1, 1024]]), tag="wv")
    if "watt" in _SKIP:
        ev_wa0 = ev_wa1 = None
    else:
        ev_wa0 = bld.dma("pool", sWA[0], watt_sb[0:64, :], D["watt"].ap(), tag="watt0")
        ev_wa1 = bld.dma("pool", sWA[1], watt_sb[64:128, :], D["watt"].ap(), tag="watt1")
    ev_wvb = bld.dma("sp", sVB, wvb32[:], D["wv_b"].ap(), tag="wvb")
    _PCH = 32768
    for (nm_, dst0_, n_) in (("outw_t", OUTW_OFF, 1024 * 1024),
                             ("w1t", W1_OFF, 2 * 44 * 8 * 128 * 128),
                             ("w2t", W2_OFF, 2 * 8 * 22 * 128 * 128)):
        for k_ in range(0, n_, _PCH):
            w_ = min(_PCH, n_ - k_)
            bld.dma("pool", sPC, AP(wscr, dst0_ + k_, [[1, w_]]),
                    AP(D[nm_], k_, [[1, w_]]), tag=f"pc_{nm_}_{k_ // _PCH}")
    ev_pc = (sPC, sPC.count)
    if "hilo" in _SKIP:
        bld.emit("dve", lambda e: e.memset(wvb2[0:1, :], 0.0), waits=[ev_wvb], tag="wvbh")
    else:
        bld.emit("dve", lambda e: e.tensor_copy(wvb2[0:1, 0:1024], wvb32[:]), waits=[ev_wvb], tag="wvbh")
    bld.emit("dve", lambda e: e.tensor_copy(wvbh32[:], wvb2[0:1, 0:1024]), tag="wvbh32")
    ev_h3 = bld.emit("dve", lambda e: e.tensor_sub(wvb2[0:1, 1024:2048], wvb32[:], wvbh32[:]), inc=sV, tag="wvblo")
    if "vones" in _SKIP:
        ev_vones = bld.emit("dve", lambda e: e.memset(vtm[:, 0:64], 1.0), inc=sV, tag="vones")
    else:
        ev_vones = bld.emit("dve", lambda e: e.memset(AP(vtm, 64, [[32 * 1040, 128], [65, 512], [1, 1]]), 1.0),
                            inc=sV, tag="vones")

    if stage == "Z1":
        import os as _os1
        if _os1.environ.get("K_Z1MM", "0") == "1":
            ev_m1 = bld.emit("pe", lambda e: e.matmul(ps_a[:, 0:512], x_bf[:, 1:129],
                                                      wv_sb[:, 0:512], start=True, stop=True),
                             waits=[ev_xbf[0], ev_wv], inc=sE, tag="zmm")
            bld.emit("dve", lambda e: e.tensor_copy(vtm[:, 0:512], ps_a[:, 0:512]),
                     waits=[ev_m1], inc=sV, tag="zcp")
        bld.barrier(SEMS, tag="z1")
        return finish_dump_bf16(x_bf[:, 0:32768])
    import os as _os
    _NT = int(_os.environ.get("K_VX_NT", "32"))
    _NOBIAS = _os.environ.get("K_VX_NOBIAS", "0") == "1"
    _NOEVAC = _os.environ.get("K_VX_NOEVAC", "0") == "1"
    r_evac = Ring()
    for ti in range(_NT):
        for eh in range(2):
            idx = 2 * ti + eh
            sl = idx % 4
            _NMM = int(_os.environ.get("K_VX_NMM", str(NCH)))
            for k in range(_NMM):
                w_ = [ev_xbf[k], ev_wv]
                if k == 0:
                    w_.append(r_evac.get(idx - 4))
                bld.emit("pe", (lambda sl_, k_, ti_, eh_: (lambda e: e.matmul(
                    ps_a[:, sl_ * 512:(sl_ + 1) * 512],
                    xbf_ap(k_, ti_ * 128, 128),
                    wv_sb[:, k_ * 1024 + eh_ * 512: k_ * 1024 + eh_ * 512 + 512],
                    start=(k_ == 0), stop=(_NOBIAS and k_ == _NMM - 1))))(sl, k, ti, eh),
                    waits=w_, tag=f"vx{idx}k{k}")
            if not _NOBIAS:
                bld.emit("pe", (lambda sl_, eh_: (lambda e: e.matmul(
                    ps_a[:, sl_ * 512:(sl_ + 1) * 512], ones_bf[0:1, 0:128],
                    wvb2[0:1, eh_ * 512: eh_ * 512 + 512], start=False, stop=False)))(sl, eh),
                    waits=[ev_h3, ev_ones], tag=f"vxbh{idx}")
                evm = bld.emit("pe", (lambda sl_, eh_: (lambda e: e.matmul(
                    ps_a[:, sl_ * 512:(sl_ + 1) * 512], ones_bf[0:1, 0:128],
                    wvb2[0:1, 1024 + eh_ * 512: 1024 + eh_ * 512 + 512], start=False, stop=True)))(sl, eh),
                    inc=sE, tag=f"vxbl{idx}")
            else:
                evm = bld.emit("pe", lambda e: e.nop(), waits=[ev_h3, ev_ones], inc=sE, tag=f"vxnop{idx}")
            if _NOEVAC:
                dst = vtm[:, (idx % 32) * 512:(idx % 32) * 512 + 512]
                ev = bld.emit("dve", (lambda sl_, dst_: (lambda e: e.tensor_copy(
                    dst_, ps_a[:, sl_ * 512:(sl_ + 1) * 512])))(sl, dst),
                    waits=[evm], inc=sV, tag=f"vxe{idx}")
            else:
                dst = AP(vtm, ti * 1040 + eh * 520, [[32 * 1040, 128], [65, 8], [1, 64]])
                ev = bld.emit("dve", (lambda sl_, dst_: (lambda e: e.tensor_copy(
                    dst_, ps_a[:, sl_ * 512:(sl_ + 1) * 512].rearrange("p (h d) -> p h d", d=64))))(sl, dst),
                    waits=[evm], inc=sV, tag=f"vxe{idx}")
            r_evac.set(idx, ev)
    ev_vx_done = ev

    if stage == "A0":
        bld.barrier(SEMS, tag="a0")
        return finish_dump_bf16(x_bf[:, 0:32768])
    if stage == "A1":
        bld.barrier(SEMS, tag="a1")
        ev_o = bld.dma("pool", sP, AP(out_fm, 0, [[32768, 128], [1024, 32], [1, 1024]]),
                       AP(vtm, 0, [[32 * 1040, 128], [1040, 32], [1, 1024]]), tag="dumpv")
        bld.wait_only("pool", [ev_o], tag="final")
        return finish(nc)

    _A2NIT = int(_os0.environ.get("K_A2_NIT", "128"))
    _A2NONORM = _os0.environ.get("K_A2_NONORM", "0") == "1"
    _A2NOPV = _os0.environ.get("K_A2_NOPV", "0") == "1"
    _A2NOEXP = _os0.environ.get("K_A2_NOEXP", "0") == "1"
    r_exp = Ring(); r_pv = Ring(); r_norm = Ring(); r_strd = Ring(); r_xwe = Ring(); r_bce = Ring()
    r_d1 = Ring()
    pend_norm = [None]
    ev_zx = bld.emit("dve", lambda e: e.memset(xw_sb[64:128, 0:512], 0.0), tag="zx0")
    ev_zx = bld.emit("dve", lambda e: e.memset(xw_sb[0:64, 512:1024], 0.0), inc=sV, tag="zx1")
    def a2_flush_norm():
        if pend_norm[0] is None:
            return
        it_, o_acc_, dst_, ev_pv_ = pend_norm[0]
        pend_norm[0] = None
        bld.emit("dve", (lambda o_: (lambda e: e.tensor_copy(rowf[0:1, 0:512], o_[64:65, :])))(o_acc_),
                 waits=[ev_pv_, r_d1.get(it_ - 1)], tag=f"nr1_{it_}")
        ev_rec = bld.emit("dve", lambda e: e.reciprocal(rowf[32:33, 0:512], rowf[0:1, 0:512]),
                          inc=sV, tag=f"nr2_{it_}")
        sl = it_ % 8
        ev_d1 = bld.dma("sp", sN1, AP(x1d, 1024 * T + sl * 512, [[512, 1], [1, 512]]), rowf[32:33, 0:512],
                        waits=[ev_rec], tag=f"nrd1_{it_}")
        r_d1.set(it_, ev_d1)
        ev_d2 = bld.dma("sp", sN2, rowf[0:64, 512:1024], AP(x1d, 1024 * T + sl * 512, [[0, 64], [1, 512]]),
                        waits=[ev_d1, r_bce.get(it_ - 1)], tag=f"nrd2_{it_}")
        ev_nm = bld.emit("dve", (lambda dst2, o_, evd: (lambda e: e.tensor_mul(
            dst2, o_[0:64, :], rowf[0:64, 512:1024])))(dst_, o_acc_, ev_d2),
            waits=[ev_d2], inc=sV, tag=f"norm_{it_}")
        r_bce.set(it_, ev_nm)
        r_norm.set(it_, ev_nm)
    it = 0
    ptf = pt_t[:].bitcast(F32)
    def emit_xw(it_):
        q_, h_ = divmod(it_, HO)
        b_ = 64 * (h_ % 2)
        ch_ = h_ // 2
        xsl_ = it_ % 2
        mm = bld.emit("pe", (lambda: (lambda e: e.matmul(
            ptf[:, 0:512], watt_sb[b_:b_ + 64, h_ * 128:(h_ + 1) * 128],
            xbf_ap(ch_, q_ * SW, SW, base=b_, np_=64), start=True, stop=True)))(),
            waits=[ev_wa0, ev_wa1, r_xwe.get(it_ - 1)], inc=sE, tag=f"xw{it_}")
        ev = bld.emit("dve", (lambda: (lambda e: e.tensor_copy(
            xw_sb[b_:b_ + 64, xsl_ * 512:(xsl_ + 1) * 512], ptf[b_:b_ + 64, 0:512])))(),
            waits=[mm, r_strd.get(it_ - 2), ev_vx_done, ev_zx], inc=sV, tag=f"xwe{it_}")
        r_xwe.set(it_, ev)
        return ev
    if _A2NIT > 0:
        emit_xw(0)
    for q in range(NSTRIP):
        if it >= _A2NIT:
            break
        for h in range(HO):
            if it >= _A2NIT:
                break
            bse = 64 * (h % 2)
            ch = h // 2
            o_acc = ps_b if it % 2 == 0 else ps_c
            xsl = it % 2
            ev_xw = r_xwe.get(it)
            ev_pv = None
            for p in range(17):
                if p == 1 and it + 1 < _A2NIT:
                    emit_xw(it + 1)
                if p < 16:
                    pi = it * 16 + p
                    for jj in range(2):
                        j = 2 * p + jj
                        sl = j % 4
                        ev_st = bld.emit("pe", (lambda b, ch_, j_, sl_, xsl_: (lambda e: e.matmul(
                            ps_a[:, sl_ * 512:(sl_ + 1) * 512],
                            xbf_ap(ch_, j_ * 128, 128, base=0, np_=128),
                            xw_sb[0:128, xsl_ * 512:(xsl_ + 1) * 512],
                            start=True, stop=True)))(bse, ch, j, sl, xsl),
                            waits=[ev_xw, r_exp.get(pi - 2)], inc=(sE if jj == 1 else None),
                            tag=f"st{it}_{j}")
                    r_strd.set(it, ev_st)
                    sl0 = (2 * p) % 4
                    if _A2NOEXP:
                        ev_exp = bld.emit("act", (lambda p_, sl0_: (lambda e: e.activation(
                            pq_buf[:, (p_ % 2) * 1024:(p_ % 2) * 1024 + 1024],
                            ps_a[:, sl0_ * 512: sl0_ * 512 + 1024],
                            AF.Copy, bias=CC.col(0.0), scale=CC.col(0.125))))(p, sl0),
                            waits=[ev_st, ev_cc, r_pv.get(pi - 2)], inc=sA, tag=f"expc{it}_{p}")
                    else:
                        ev_exp = bld.emit("act", (lambda p_, sl0_: (lambda e: e.activation(
                            pq_buf[:, (p_ % 2) * 1024:(p_ % 2) * 1024 + 1024],
                            ps_a[:, sl0_ * 512: sl0_ * 512 + 1024],
                            AF.Exp, bias=CC.col(-C_OUTER), scale=CC.col(0.125))))(p, sl0),
                            waits=[ev_st, ev_cc, r_pv.get(pi - 2)], inc=sA, tag=f"exp{it}_{p}")
                    r_exp.set(pi, ev_exp)
                if p >= 1 and not _A2NOPV:
                    pp = p - 1
                    ppi = it * 16 + pp
                    for jj in range(2):
                        j = 2 * pp + jj
                        w_ = [r_exp.get(ppi), ev_vones]
                        if j == 0:
                            w_.append(r_norm.get(it - 2))
                        ev_pv = bld.emit("pe", (lambda h_, j_, pp_, o_: (lambda e: e.matmul(
                            o_[0:65, :],
                            AP(vtm, j_ * 1040 + h_ * 65, [[32 * 1040, 128], [1, 65]]),
                            pq_buf[:, ((pp_ % 2) * 2 + (j_ % 2)) * 512: ((pp_ % 2) * 2 + (j_ % 2)) * 512 + 512],
                            start=(j_ == 0), stop=(j_ == 31))))(h, j, pp, o_acc),
                            waits=w_, inc=(sE if jj == 1 else None), tag=f"pv{it}_{j}")
                    r_pv.set(ppi, ev_pv)
            if _A2NOPV:
                ev_pv = bld.emit("pe", (lambda o_: (lambda e: e.matmul(
                    o_[0:65, :], AP(vtm, 0 * 1040, [[32 * 1040, 128], [1, 65]]),
                    pq_buf[:, 0:512], start=True, stop=True)))(o_acc),
                    waits=[r_exp.get(it * 16), r_norm.get(it - 2), ev_vones], inc=sE, tag=f"pv1_{it}")
            if _A2NONORM:
                dst0 = o_buf[bse:bse + 64, ch * 4096 + q * SW: ch * 4096 + q * SW + SW]
                ev_nm = bld.emit("dve", (lambda dst_, o_: (lambda e: e.tensor_copy(
                    dst_, o_[0:64, :])))(dst0, o_acc), waits=[ev_pv], inc=sV, tag=f"nmc_{it}")
                r_norm.set(it, ev_nm)
                it += 1
                continue
            a2_flush_norm()
            dst = o_buf[bse:bse + 64, ch * 4096 + q * SW: ch * 4096 + q * SW + SW]
            pend_norm[0] = (it, o_acc, dst, ev_pv)
            it += 1
    a2_flush_norm()
    bld.barrier(SEMS, tag="a2end")
    if stage == "A2":
        return finish_dump_bf16(o_buf[:, 0:32768])

    ev_z = bld.emit("dve", lambda e: e.memset(AP(stream, 0, [[8 * CST, 128], [CST, 8], [1, 1]]), 0.0),
                    inc=sV, tag="zcol")
    r_ow = Ring(); r_stt = Ring(); r_stat = Ring(); r_app = Ring(); r_mu = Ring()
    for m in range(NCH):
        osl = m % 2
        src = AP(wscr, OUTW_OFF + m * 128, [[1024, 128], [128 * 1024, 8], [1, 128]])
        ev_ow = bld.dma("pool", sOW[osl], outw_buf[:, osl * 1024: osl * 1024 + 1024], src,
                        waits=[r_ow.get(m - 2), ev_pc], tag=f"ow{m}")
        for s in range(NSTRIP):
            im = m * 8 + s
            bank = ps_b if im % 2 == 0 else ps_c
            evp = None
            for k in range(NCH):
                evp = bld.emit("pe", (lambda bank_, osl_, k_, s_: (lambda e: e.matmul(
                    bank_[:], outw_buf[:, osl_ * 1024 + k_ * 128: osl_ * 1024 + k_ * 128 + 128],
                    o_buf[:, k_ * 4096 + s_ * SW: k_ * 4096 + s_ * SW + SW],
                    start=(k_ == 0), stop=(k_ == 7))))(bank, osl, k, s),
                    waits=[ev_ow, r_norm.get(s * 16 + 15), r_stt.get(im - 2)],
                    inc=(sE if k == 7 else None), tag=f"op{im}")
            if s == NSTRIP - 1:
                r_ow.set(m, evp)
            ev_x0 = bld.dma("sp", sX0, x0_buf[:], D["x_fm"][m * 128:(m + 1) * 128, s * SW:(s + 1) * SW],
                            waits=[r_stt.get(im - 1)], tag=f"x0{im}")
            ev_stt = bld.emit("dve", (lambda bank_, m_, s_: (lambda e: e.scalar_tensor_tensor(
                st_ap(m_, s_ * SW, SW), bank_[:], outb_sb[:, m_:m_ + 1], x0_buf[:],
                ALU.add, ALU.add)))(bank, m, s),
                waits=[evp, ev_x0, ev_z, ev_outb], inc=sV, tag=f"stt{im}")
            r_stt.set(im, ev_stt)
    for s in range(NSTRIP):
        evm1 = evm2 = None
        for c in range(NCH):
            ic = s * 8 + c
            csl = c % 2
            bld.emit("act", (lambda c_, s_, csl_: (lambda e: e.activation(
                s1bf[:, csl_ * 512:(csl_ + 1) * 512], st_ap(c_, s_ * SW, SW),
                AF.Copy, bias=0.0, scale=1.0)))(c, s, csl),
                waits=[r_stt.get(c * 8 + s), r_stat.get(ic - 2)], tag=f"sb{ic}")
            ev_c2 = bld.emit("act", (lambda c_, s_, csl_: (lambda e: e.activation(
                sqbf[:, csl_ * 512:(csl_ + 1) * 512], st_ap(c_, s_ * SW, SW),
                AF.Square, bias=CC.col(0.0), scale=CC.col(1.0))))(c, s, csl),
                inc=sA, tag=f"sq{ic}")
            evm1 = bld.emit("pe", (lambda c_, csl_: (lambda e: e.matmul(
                ps_d[0:1, :], ones_bf[:, 0:1], s1bf[:, csl_ * 512:(csl_ + 1) * 512],
                start=(c_ == 0), stop=(c_ == 7))))(c, csl),
                waits=[ev_c2, r_mu.get(s - 1)], tag=f"sm1{ic}")
            evm2 = bld.emit("pe", (lambda c_, csl_: (lambda e: e.matmul(
                ps_a[0:1, 0:512], ones_bf[:, 0:1], sqbf[:, csl_ * 512:(csl_ + 1) * 512],
                start=(c_ == 0), stop=(c_ == 7))))(c, csl), inc=sE, tag=f"sm2{ic}")
            r_stat.set(ic, evm2)
        r0 = rowf[0:1, 0:512]; r32 = rowf[32:33, 0:512]
        r64 = rowf[64:65, 0:512]; r64b = rowf[64:65, 512:1024]
        bld.emit("dve", lambda e: e.tensor_scalar(r0, ps_d[0:1, :], CC.col(1.0 / 1024.0, 0, 1),
                                                  None, ALU.mult), waits=[evm2], tag=f"mu{s}")
        ev_mu = bld.emit("dve", lambda e: e.tensor_scalar(r32, ps_a[0:1, 0:512],
                                                          CC.col(1.0 / 1024.0, 0, 1), None, ALU.mult),
                         inc=sV, tag=f"e2{s}")
        r_mu.set(s, ev_mu)
        bld.emit("dve", lambda e: e.tensor_mul(r64, r0, r0), tag=f"mu2{s}")
        bld.emit("dve", lambda e: e.tensor_copy(r64b, r32), tag=f"e2c{s}")
        ev_var = bld.emit("dve", lambda e: e.tensor_sub(r32, r64b, r64), inc=sV, tag=f"var{s}")
        bld.emit("act", lambda e: e.activation(r64, r32, AF.Ln,
                                               bias=CC.col(1e-5, 32, 1), scale=CC.col(1.0, 32, 1)),
                 waits=[ev_var], tag=f"ln{s}")
        ev_rstd = bld.emit("act", lambda e: e.activation(r32, r64, AF.Exp,
                                                         bias=CC.col(0.0, 64, 1), scale=CC.col(-0.5, 64, 1)),
                           inc=sA, tag=f"rstd{s}")
        bld.emit("dve", lambda e: e.tensor_copy(r64, r32), waits=[ev_rstd], tag=f"rsc{s}")
        bld.emit("dve", lambda e: e.tensor_copy(r64b, r0), tag=f"muc{s}")
        bld.emit("dve", lambda e: e.tensor_mul(r0, r64, r64b), tag=f"r2a{s}")
        bld.emit("dve", lambda e: e.tensor_scalar(r0, r0, CC.col(-1.0, 0, 1), None, ALU.mult),
                 tag=f"r2b{s}")
        bld.emit("dve", lambda e: e.tensor_copy(rowb[0:1, 0:512], r32), tag=f"hl1{s}")
        bld.emit("dve", lambda e: e.tensor_copy(r64b, rowb[0:1, 0:512]), tag=f"hl2{s}")
        ev_hl = bld.emit("dve", lambda e: e.tensor_sub(rowb[0:1, 512:1024], r64, r64b),
                         inc=sV, tag=f"hl3{s}")
        bld.emit("pe", lambda e: e.matmul(ps_a[:, 512:1024], ones_bf[0:1, 0:128], rowb[0:1, 0:512],
                                          start=True, stop=False), waits=[ev_hl, r_app.get(s - 1)],
                 tag=f"rb1{s}")
        ev_rb2 = bld.emit("pe", lambda e: e.matmul(ps_a[:, 512:1024], ones_bf[0:1, 0:128],
                                                   rowb[0:1, 512:1024], start=False, stop=True),
                          inc=sE, tag=f"rb2{s}")
        bld.emit("dve", lambda e: e.tensor_copy(rowb[0:1, 0:512], r0), waits=[ev_rb2], tag=f"hl4{s}")
        bld.emit("dve", lambda e: e.tensor_copy(r64, rowb[0:1, 0:512]), tag=f"hl5{s}")
        bld.emit("dve", lambda e: e.tensor_copy(r64b, r0), tag=f"hl5b{s}")
        ev_hl2 = bld.emit("dve", lambda e: e.tensor_sub(rowb[0:1, 512:1024], r64b, r64),
                          inc=sV, tag=f"hl6{s}")
        bld.emit("pe", lambda e: e.matmul(ps_a[:, 1024:1536], ones_bf[0:1, 0:128], rowb[0:1, 0:512],
                                          start=True, stop=False), waits=[ev_hl2], tag=f"rb3{s}")
        ev_bc = bld.emit("pe", lambda e: e.matmul(ps_a[:, 1024:1536], ones_bf[0:1, 0:128],
                                                  rowb[0:1, 512:1024], start=False, stop=True),
                         inc=sE, tag=f"rb4{s}")
        ev_app = None
        for c in range(NCH):
            bld.emit("dve", (lambda c_, s_: (lambda e: e.tensor_mul(
                st_ap(c_, s_ * SW, SW), st_ap(c_, s_ * SW, SW), ps_a[:, 512:1024])))(c, s),
                waits=[ev_bc], tag=f"ap1{s}_{c}")
            ev_app = bld.emit("dve", (lambda c_, s_: (lambda e: e.tensor_add(
                st_ap(c_, s_ * SW, SW), st_ap(c_, s_ * SW, SW), ps_a[:, 1024:1536])))(c, s),
                inc=sV, tag=f"ap2{s}_{c}")
        r_app.set(s, ev_app)
        bld.dma("sp", sS, AP(x1d, s * SW, [[T, 128], [128 * T, 8], [1, SW]]),
                AP(stream, 1 + s * SW, [[8 * CST, 128], [CST, 8], [1, SW]]),
                waits=[ev_app], tag=f"spill{s}")
    bld.barrier(SEMS, tag="a3end")
    if _os0.environ.get("K_D_NOLN", "0") == "4":
        ev_o = bld.dma("sp", sS, AP(out_fm, 0, [[T, 128], [1, T]]),
                       AP(x1d, 0, [[T, 128], [1, T]]), tag="x1dump")
        for c in range(1, 8):
            ev_o = bld.dma("sp", sS, AP(out_fm, c * 128 * T, [[T, 128], [1, T]]),
                           AP(x1d, c * 128 * T, [[T, 128], [1, T]]), tag=f"x1dump{c}")
        bld.wait_only("sp", [ev_o], tag="final")
        return finish(nc)

    def finish_dump_stream():
        ev_o = bld.dma("sp", sS, AP(out_fm, 0, [[T, 128], [128 * T, 8], [1, T]]),
                       AP(stream, 1, [[8 * CST, 128], [CST, 8], [1, T]]), tag="outS")
        bld.wait_only("sp", [ev_o], tag="final")
        return finish(nc)

    if stage == "A":
        return finish_dump_stream()

    ev_id2 = bld.dma("pool", sID, ident_sb[:], D["ident"].ap(), tag="ident2")
    ev_mk = bld.dma("pool", sMK, masks_sb[:],
                    AP(D["masks"], 0, [[128, 128], [128 * 128, 2], [1, 128]]), tag="masks")

    def emit_rows_recip(o_sum_ap, waits, tagp):
        bld.emit("dve", (lambda a_: (lambda e: e.tensor_copy(rowf[0:1, 0:512], a_)))(o_sum_ap),
                 waits=waits, tag=tagp + "r1")
        bld.emit("dve", lambda e: e.reciprocal(rowf[32:33, 0:512], rowf[0:1, 0:512]), tag=tagp + "r2")
        bld.emit("dve", lambda e: e.tensor_copy(rowb[0:1, 0:512], rowf[32:33, 0:512]), tag=tagp + "r3")
        bld.emit("dve", lambda e: e.tensor_copy(rowf[64:65, 0:512], rowb[0:1, 0:512]), tag=tagp + "r4")
        bld.emit("dve", lambda e: e.tensor_copy(rowf[64:65, 512:1024], rowf[32:33, 0:512]), tag=tagp + "r5")
        return bld.emit("dve", lambda e: e.tensor_sub(rowb[0:1, 512:1024], rowf[64:65, 512:1024],
                                                      rowf[64:65, 0:512]), inc=sV, tag=tagp + "r6")

    def emit_bcast_psd(ev_rows, extra_waits, tagp):
        bld.emit("pe", lambda e: e.matmul(ps_d[0:64, :], ones_bf[0:1, 0:64], rowb[0:1, 0:512],
                                          start=True, stop=False), waits=[ev_rows] + extra_waits,
                 tag=tagp + "b1")
        evb = bld.emit("pe", lambda e: e.matmul(ps_d[0:64, :], ones_bf[0:1, 0:64], rowb[0:1, 512:1024],
                                                start=False, stop=True), inc=sE, tag=tagp + "b2")
        return bld.emit("dve", lambda e: e.tensor_copy(rowf[0:64, 0:512], ps_d[0:64, :]),
                        waits=[evb], inc=sV, tag=tagp + "be")

    r0 = rowf[0:1, 0:512]; r32 = rowf[32:33, 0:512]
    r64 = rowf[64:65, 0:512]; r64b = rowf[64:65, 512:1024]

    def emit_ln_rows(sum_ap, sumsq_ap, waits, tagp):
        bld.emit("dve", (lambda a_: (lambda e: e.tensor_scalar(
            r0, a_, CC.col(1.0 / 1024.0, 0, 1), None, ALU.mult)))(sum_ap), waits=waits, tag=tagp + "mu")
        bld.emit("dve", (lambda a_: (lambda e: e.tensor_scalar(
            r32, a_, CC.col(1.0 / 1024.0, 0, 1), None, ALU.mult)))(sumsq_ap), tag=tagp + "e2")
        bld.emit("dve", lambda e: e.tensor_mul(r64, r0, r0), tag=tagp + "mu2")
        bld.emit("dve", lambda e: e.tensor_copy(r64b, r32), tag=tagp + "e2c")
        ev_var = bld.emit("dve", lambda e: e.tensor_sub(r32, r64b, r64), inc=sV, tag=tagp + "var")
        emit_ln_rows.last_var = ev_var
        bld.emit("act", lambda e: e.activation(r64, r32, AF.Ln, bias=CC.col(1e-5, 32, 1),
                                               scale=CC.col(1.0, 32, 1)), waits=[ev_var], tag=tagp + "ln")
        ev_rstd = bld.emit("act", lambda e: e.activation(r32, r64, AF.Exp, bias=CC.col(0.0, 64, 1),
                                                         scale=CC.col(-0.5, 64, 1)), inc=sA,
                           tag=tagp + "rstd")
        bld.emit("dve", lambda e: e.tensor_copy(r64, r32), waits=[ev_rstd], tag=tagp + "rsc")
        bld.emit("dve", lambda e: e.tensor_copy(r64b, r0), tag=tagp + "muc")
        bld.emit("dve", lambda e: e.tensor_mul(r0, r64, r64b), tag=tagp + "r2a")
        bld.emit("dve", lambda e: e.tensor_scalar(r0, r0, CC.col(-1.0, 0, 1), None, ALU.mult),
                 tag=tagp + "r2b")
        bld.emit("dve", lambda e: e.tensor_copy(rowb[0:1, 0:512], r32), tag=tagp + "h1")
        bld.emit("dve", lambda e: e.tensor_copy(r64b, rowb[0:1, 0:512]), tag=tagp + "h2")
        ev_hl = bld.emit("dve", lambda e: e.tensor_sub(rowb[0:1, 512:1024], r64, r64b),
                         inc=sV, tag=tagp + "h3")
        bld.emit("pe", lambda e: e.matmul(ps_a[:, 512:1024], ones_bf[0:1, 0:128], rowb[0:1, 0:512],
                                          start=True, stop=False), waits=[ev_hl], tag=tagp + "p1")
        ev_rb2 = bld.emit("pe", lambda e: e.matmul(ps_a[:, 512:1024], ones_bf[0:1, 0:128],
                                                   rowb[0:1, 512:1024], start=False, stop=True),
                          inc=sE, tag=tagp + "p2")
        bld.emit("dve", lambda e: e.tensor_copy(rowb[0:1, 0:512], r0), waits=[ev_rb2], tag=tagp + "h4")
        bld.emit("dve", lambda e: e.tensor_copy(r64, rowb[0:1, 0:512]), tag=tagp + "h5")
        bld.emit("dve", lambda e: e.tensor_copy(r64b, r0), tag=tagp + "h5b")
        ev_hl2 = bld.emit("dve", lambda e: e.tensor_sub(rowb[0:1, 512:1024], r64b, r64),
                          inc=sV, tag=tagp + "h6")
        bld.emit("pe", lambda e: e.matmul(ps_a[:, 1024:1536], ones_bf[0:1, 0:128], rowb[0:1, 0:512],
                                          start=True, stop=False), waits=[ev_hl2], tag=tagp + "p3")
        return bld.emit("pe", lambda e: e.matmul(ps_a[:, 1024:1536], ones_bf[0:1, 0:128],
                                                 rowb[0:1, 512:1024], start=False, stop=True),
                        inc=sE, tag=tagp + "p4")

    LN8 = float(np.log(8.0))
    _dmerge = (stage == "full") and _os0.environ.get("K_DM_OFF", "0") != "1"
    r_x1 = Ring()
    for lyr in range(2):
        tg = f"L{lyr}"
        ev_wqkv = bld.dma("pool", sWQ, wqkv_sb[:],
                          AP(D["wqkv"], lyr * 1024 * 1536, [[1536, 128], [128 * 1536, 8], [1, 1536]]),
                          tag=tg + "wqkv")
        ev_wo = bld.dma("pool", sWO, wo_sb[:],
                        AP(D["wo"], lyr * 512 * 1024, [[1024, 128], [128 * 1024, 4], [1, 1024]]),
                        tag=tg + "wo")
        ev_qs = bld.dma("sp", sQS, qsks[:, 0:64], AP(D["qs_t"], lyr * 128 * 64, [[64, 128], [1, 64]]),
                        tag=tg + "qs")
        ev_ks = bld.dma("sp", sKS, qsks[:, 64:128], AP(D["ks_t"], lyr * 128 * 64, [[64, 128], [1, 64]]),
                        tag=tg + "ks")
        ev_lc = []
        for (name, col0, w_) in (("lng_col", 0, 8), ("lnb_col", 8, 8), ("b1a_col", 16, 22),
                                 ("b1g_col", 38, 22), ("b2_col", 60, 8)):
            ev_lc.append(bld.dma("sp", sLC[len(ev_lc)], lcols[:, col0:col0 + w_],
                                 AP(D[name], lyr * 128 * w_, [[w_, 128], [1, w_]]), tag=tg + name))
        ev_vlo = bld.emit("dve", lambda e: e.memset(AP(vloc, 64, [[4160, 128], [65, 64], [1, 1]]), 1.0),
                          inc=sV, tag=tg + "vlones")
        ev_bnd = bld.emit("dve", lambda e: e.tensor_copy(
            AP(bound, 0, [[64, 128], [7, 8], [1, 7]]),
            AP(stream, 1 + 511, [[8 * CST, 128], [CST, 8], [512, 7]])), inc=sV, tag=tg + "bnd")

        _BNB = int(_os0.environ.get("K_B_NB", "8"))
        _BQONLY = _os0.environ.get("K_B_QKV_ONLY", "0") == "1"
        _BNOWO = _os0.environ.get("K_B_NOWO", "0") == "1"
        r_hb = Ring(); r_qkv = Ring(); r_vev = Ring(); r_tr = Ring(); r_qkev = Ring()
        r_expL = Ring(); r_pvL = Ring(); r_nmL = Ring(); r_stL = Ring(); r_wo = Ring(); r_rot = Ring()
        r_bceL = Ring(); r_d1L = Ring(); r_qch = Ring()
        pendL = [None]
        def b_flush_norm():
            if pendL[0] is None:
                return
            b_, h_, ev_pv_, oa = pendL[0]
            pendL[0] = None
            idx = b_ * 8 + h_
            ev_cp1 = bld.emit("dve", (lambda oa_: (lambda e: e.tensor_copy(
                rowf[0:1, 0:512], oa_[64:65, :])))(oa),
                waits=[ev_pv_], inc=sV, tag=f"{tg}nr0_{b_}_{h_}")
            bld.emit("act", lambda e: e.activation(
                rowf[96:97, 0:512], rowf[0:1, 0:512], AF.Ln,
                bias=CC.col(0.0, 0, 1), scale=CC.col(1.0, 0, 1)),
                waits=[ev_cp1], tag=f"{tg}nr1_{b_}_{h_}")
            ev_rec = bld.emit("act", lambda e: e.activation(
                rowf[32:33, 0:512], rowf[96:97, 0:512], AF.Exp,
                bias=CC.col(0.0, 96, 1), scale=CC.col(-1.0, 96, 1)),
                waits=[r_d1L.get(idx - 1)], inc=sA, tag=f"{tg}nr2_{b_}_{h_}")
            sl = idx % 8
            ev_d1 = bld.dma("sp", sN1, AP(x1d, 1024 * T + sl * 512, [[512, 1], [1, 512]]), rowf[32:33, 0:512],
                            waits=[ev_rec], tag=f"{tg}nrd1_{b_}_{h_}")
            r_d1L.set(idx, ev_d1)
            ev_d2 = bld.dma("sp", sN2, rowf[0:64, 512:1024], AP(x1d, 1024 * T + sl * 512, [[0, 64], [1, 512]]),
                            waits=[ev_d1, r_bceL.get(idx - 1)], tag=f"{tg}nrd2_{b_}_{h_}")
            hb_ = 64 * (h_ % 2)
            dst2 = oloc[hb_:hb_ + 64, (h_ // 2) * 512:(h_ // 2 + 1) * 512]
            ev_nm = bld.emit("dve", (lambda d3, oa_: (lambda e: e.tensor_mul(
                d3, oa_[0:64, :], rowf[0:64, 512:1024])))(dst2, oa),
                waits=[ev_d2], inc=sV, tag=f"{tg}nm{b_}_{h_}")
            r_bceL.set(idx, ev_nm)
            r_nmL.set((b_, h_), ev_nm)
        for b in range(_BNB):
            t0b = b * 512
            bpar = b % 2
            for ti in range(4):
                t0 = t0b + ti * 128
                rsl = ti % 2
                ev_rt = []
                for tbl, nm in enumerate(("rot_csq", "rot_ssq", "rot_csk", "rot_ssk")):
                    ev_rt.append(bld.dma("sp", sR[tbl * 2 + rsl],
                                         rotb[:, rsl * 256 + tbl * 64: rsl * 256 + tbl * 64 + 64],
                                         AP(D[nm], t0 * 64, [[64, 128], [1, 64]]),
                                         waits=[r_rot.get(b * 4 + ti - 2)], tag=f"{tg}rot{b}_{ti}_{tbl}"))
                ev_rot = ev_rt
                bld.emit("dve", (lambda t_: (lambda e: e.tensor_copy(
                    hbuf[:, 0:512].rearrange("p (c t) -> p c t", t=128),
                    AP(stream, 1 + t_, [[8 * CST, 128], [CST, 4], [1, 128]]))))(t0),
                    waits=[r_qkv.get((b, ti - 1)), r_hb.get((b, ti - 1)), ev_bnd],
                    tag=f"{tg}hbA{b}_{ti}")
                bld.emit("dve", (lambda t_: (lambda e: e.tensor_copy(
                    hbuf[:, 512:1024].rearrange("p (c t) -> p c t", t=128),
                    AP(stream, 1 + t_ - 1 + 4 * CST, [[8 * CST, 128], [CST, 4], [1, 128]]))))(t0),
                    tag=f"{tg}hbB{b}_{ti}")
                if ti == 0 and b > 0:
                    bld.emit("dve", (lambda b_: (lambda e: e.tensor_copy(
                        AP(hbuf, 512, [[1024, 128], [128, 4], [1, 1]]),
                        AP(bound, 4 * 7 + b_ - 1, [[64, 128], [7, 4], [1, 1]]))))(b),
                        tag=f"{tg}hbb{b}")
                ev_hb = bld.emit("dve", lambda e: e.engine_nop(), inc=sV, tag=f"{tg}hbf{b}_{ti}")
                r_hb.set((b, ti), ev_hb)
                ev_q = None
                for n in range(3):
                    for k in range(NCH):
                        ev_q = bld.emit("pe", (lambda n_, k_: (lambda e: e.matmul(
                            ps_a[:, n_ * 512:(n_ + 1) * 512], hbuf[:, k_ * 128:(k_ + 1) * 128],
                            wqkv_sb[:, k_ * 1536 + n_ * 512: k_ * 1536 + n_ * 512 + 512],
                            start=(k_ == 0), stop=(k_ == NCH - 1))))(n, k),
                            waits=[ev_hb, ev_wqkv, r_vev.get((b, ti - 1)), r_qkev.get((b, ti - 1)),
                                   r_nmL.get((b - 1, 7))],
                            inc=(sE if (n == 2 and k == NCH - 1) else None), tag=f"{tg}qkv{b}_{ti}_{n}_{k}")
                r_qkv.set((b, ti), ev_q)
                ev_v = bld.emit("dve", (lambda bp_, ti_: (lambda e: e.tensor_copy(
                    AP(vloc, bp_ * 2080 + ti_ * 520, [[4160, 128], [65, 8], [1, 64]]),
                    ps_a[:, 1024:1536].rearrange("p (h d) -> p h d", d=64))))(bpar, ti),
                    waits=[ev_q], inc=sV, tag=f"{tg}vev{b}_{ti}")
                r_vev.set((b, ti), ev_v)
                ev_tq = []
                seq0 = (b * 4 + ti) * 2
                _qkset = () if _os0.environ.get("K_B_NOQK", "0") == "1" else (
                    ("q", 0, 0, 0, LN8), ("k", 1, 16, 64, 0.0))
                for wi, (which, psl, ssoff, rcol, lnb_) in enumerate(_qkset):
                    _UPTO = int(_os0.environ.get("K_B_QK_UPTO", "11"))
                    pa = ps_a[:, psl * 512:(psl + 1) * 512]
                    bld.emit("act", (lambda pa_: (lambda e: e.activation(
                        q32[:], pa_, AF.Copy, bias=0.0, scale=1.0)))(pa),
                        waits=[ev_q, r_qch.get(seq0 + wi - 1)], tag=f"{tg}{which}c{b}_{ti}")
                    ev_sq = bld.emit("act", lambda e: e.activation(
                        qt1[:], q32[:], AF.Square, bias=CC.col(0.0), scale=CC.col(1.0)),
                        inc=sA, tag=f"{tg}{which}sq{b}_{ti}")
                    ev_ss = bld.emit("dve", (lambda so_: (lambda e: e.reduce_sum(
                        l2buf[:, so_:so_ + 8], qt1.rearrange("p (h d) -> p h d", d=64),
                        axis=AX.X)))(ssoff), waits=[ev_sq], inc=sV, tag=f"{tg}{which}rs{b}_{ti}")
                    bld.emit("pe", lambda e: e.matmul(
                        ps_d[0:64, 0:64], ones_bf[:, 0:64], ones_bf[:, 0:64],
                        start=True, stop=True), waits=[ev_ss], tag=f"{tg}{which}hb{b}_{ti}")
                    if _UPTO >= 5:
                        bld.emit("act", (lambda so_: (lambda e: e.activation(
                            l2buf[:, so_ + 8:so_ + 16], l2buf[:, so_:so_ + 8], AF.Ln,
                            bias=CC.col(1e-24), scale=CC.col(1.0))))(ssoff),
                            waits=[ev_ss], tag=f"{tg}{which}ln{b}_{ti}")
                    if _UPTO >= 6:
                        ev_r = bld.emit("act", (lambda so_, lb_: (lambda e: e.activation(
                            l2buf[:, so_:so_ + 8], l2buf[:, so_ + 8:so_ + 16], AF.Exp,
                            bias=CC.col(lb_), scale=CC.col(-0.5))))(ssoff, lnb_),
                            inc=sA, tag=f"{tg}{which}ex{b}_{ti}")
                    else:
                        ev_r = ev_ss
                    if _UPTO >= 7:
                        bld.emit("dve", (lambda rc_: (lambda e: e.tensor_tensor(
                            qt1.rearrange("p (h d) -> p h d", d=64),
                            q32.rearrange("p (h d) -> p h d", d=64),
                            AP(qsks, rc_, [[128, 128], [0, 8], [1, 64]]), ALU.mult)))(rcol),
                            waits=[ev_qs, ev_ks], tag=f"{tg}{which}s{b}_{ti}")
                    cs0 = rsl * 256 + (0 if which == "q" else 128)
                    if _UPTO >= 8:
                        bld.emit("dve", (lambda c0_: (lambda e: e.tensor_tensor(
                            qt2.rearrange("p (h d) -> p h d", d=64),
                            qt1.rearrange("p (h d) -> p h d", d=64),
                            AP(rotb, c0_, [[512, 128], [0, 8], [1, 64]]), ALU.mult)))(cs0),
                            waits=ev_rot, tag=f"{tg}{which}ca{b}_{ti}")
                    if _UPTO >= 9:
                        bld.emit("dve", (lambda c0_: (lambda e: e.tensor_tensor(
                            q32.rearrange("p (h s d) -> p h s d", s=2, d=32),
                            AP(qt1, 32, [[512, 128], [64, 8], [-32, 2], [1, 32]]),
                            AP(rotb, c0_ + 64, [[512, 128], [0, 8], [32, 2], [1, 32]]), ALU.mult)))(cs0),
                            tag=f"{tg}{which}cb{b}_{ti}")
                    if _UPTO >= 10:
                        bld.emit("dve", lambda e: e.tensor_add(qt1[:], qt2[:], q32[:]), tag=f"{tg}{which}ad{b}_{ti}")
                    ev_sc = bld.emit("dve", (lambda so_: (lambda e: e.tensor_tensor(
                        qbf.rearrange("p (h d) -> p h d", d=64),
                        qt1.rearrange("p (h d) -> p h d", d=64),
                        AP(l2buf, so_, [[32, 128], [1, 8], [0, 64]]), ALU.mult)))(ssoff),
                        waits=[ev_r], inc=sV, tag=f"{tg}{which}rr{b}_{ti}")
                    r_qch.set(seq0 + wi, ev_sc)
                    if _os0.environ.get("K_B_NOTR", "0") == "1":
                        ev_tq.append(ev_sc)
                        continue
                    tb = pt_t if which == "q" else ps_b[:].bitcast(BF16)
                    ev_t = None
                    for p in range(4):
                        ev_t = bld.emit("pe", (lambda p_, tb_: (lambda e: e.transpose(
                            tb_[:, p_ * 128:(p_ + 1) * 128],
                            qbf[:, p_ * 128:(p_ + 1) * 128], ident_sb[:])))(p, tb),
                            waits=[ev_sc, ev_id2, r_tr.get((b, ti - 1, which)),
                                   r_nmL.get((b - 1, 7))],
                            inc=(sE if p == 3 else None), tag=f"{tg}{which}tr{b}_{ti}_{p}")
                    if which == "q":
                        dst = AP(qfm, ti * 128, [[4 * 512, 128], [512, 4], [1, 128]])
                    else:
                        dst = AP(kfm, bpar * 512 + ti * 128, [[8 * 512, 128], [1024, 4], [1, 128]])
                    ev_te = bld.emit("dve", (lambda dst_, tb_: (lambda e: e.tensor_copy(
                        dst_, tb_[:, 0:512].rearrange("p (s c) -> p s c", c=128))))(dst, tb),
                        waits=[ev_t], inc=sV, tag=f"{tg}{which}te{b}_{ti}")
                    r_tr.set((b, ti, which), ev_te)
                    ev_tq.append(ev_te)
                ev_qk = bld.emit("dve", lambda e: e.engine_nop(), waits=ev_tq, inc=sV, tag=f"{tg}qkf{b}_{ti}")
                r_qkev.set((b, ti), ev_qk)
                r_rot.set(b * 4 + ti, ev_qk)
            bld.emit("dve", lambda e: e.memset(qfm_z[64:128, 0:512], 0.0), tag=f"{tg}zq0_{b}")
            ev_zq = bld.emit("dve", lambda e: e.memset(qfm_z[0:64, 512:1024], 0.0),
                             inc=sV, tag=f"{tg}zq1_{b}")
            for h in range(0 if _BQONLY else HL):
                hb = 64 * (h % 2)
                pr = h // 2
                oa_h = ps_d[:] if h % 2 == 0 else pt_t[:].bitcast(F32)[:, 0:512]
                worder = [4, 0, 1, 2, 3, 5, 6, 7] if b > 0 else [4, 5, 6, 7]
                pairs = [(4, 0), (1, 2), (3, 5), (6, 7)] if b > 0 else [(4, 5), (6, 7)]
                npair = len(pairs)
                ev_qz = bld.emit("dve", (lambda hb_, pr_, h_: (lambda e: e.tensor_copy(
                    qfm_z[hb_:hb_ + 64, (h_ % 2) * 512:(h_ % 2) * 512 + 512],
                    qfm[hb_:hb_ + 64, pr_ * 512: pr_ * 512 + 512])))(hb, pr, h),
                    waits=[ev_zq, r_stL.get((b, h - 2, npair - 1))],
                    inc=sV, tag=f"{tg}qz{b}_{h}")
                ev_pv = None
                for pi, pr2 in enumerate(pairs):
                    if pi == 1:
                        b_flush_norm()
                    rg = (pi % 2) * 1024
                    info = []
                    ev_st = None
                    for k2, w in enumerate(pr2):
                        side, j = w // 4, w % 4
                        if side == 0:
                            c0, c1 = 0, 128 * (j + 1)
                        else:
                            c0, c1 = 128 * j, 512
                        kb_par = (b - 1 + side) % 2
                        colof = rg + k2 * 512
                        bld.emit("pe", (lambda h_, pr_, kp_, j_, c0_, c1_, co_: (lambda e: e.matmul(
                            ps_a[:, co_ + c0_: co_ + c1_],
                            kfm[0:128, pr_ * 1024 + kp_ * 512 + j_ * 128: pr_ * 1024 + kp_ * 512 + j_ * 128 + 128],
                            qfm_z[0:128, (h_ % 2) * 512 + c0_: (h_ % 2) * 512 + c1_],
                            start=True, stop=False)))(h, pr, kb_par, j, c0, c1, colof),
                            waits=[ev_qz, r_qkev.get((b, 3)), r_vev.get((b, 3)),
                                   r_expL.get((b, h, pi - 2)),
                                   (r_expL.get((b, h - 1, pi - 2 + npair)) if pi < 2 else None)],
                            tag=f"{tg}st{b}_{h}_{w}")
                        ev_st = bld.emit("pe", (lambda j_, side_, co_: (lambda e: e.matmul(
                            ps_a[:, co_ + 128 * j_: co_ + 128 * j_ + 128],
                            masks_sb[:, side_ * 128:(side_ + 1) * 128], ident_sb[:, 0:128],
                            start=False, stop=True)))(j, side, colof),
                            waits=[ev_mk, ev_id2], inc=(sE if k2 == 1 else None),
                            tag=f"{tg}stm{b}_{h}_{w}")
                        info.append((side, j, c0, c1, kb_par, colof))
                    r_stL.set((b, h, pi), ev_st)
                    ev_ex = bld.emit("act", (lambda rg_, pi_: (lambda e: e.activation(
                        ploc[:, (pi_ % 2) * 1024: (pi_ % 2) * 1024 + 1024], ps_a[:, rg_:rg_ + 1024],
                        AF.Exp, bias=CC.col(-C_LOCAL), scale=CC.col(1.0))))(rg, pi),
                        waits=[ev_st, r_pvL.get((b, h, pi - 2)), ev_cc], inc=sA,
                        tag=f"{tg}ex{b}_{h}_{pi}")
                    r_expL.set((b, h, pi), ev_ex)
                    for k2, (side, j, c0, c1, kb_par, colof) in enumerate(info):
                        ev_pv = bld.emit("pe", (lambda h_, kp_, j_, c0_, c1_, co_, first_, last_, oa_: (lambda e: e.matmul(
                            oa_[0:65, c0_:c1_],
                            AP(vloc, kp_ * 2080 + j_ * 520 + h_ * 65, [[4160, 128], [1, 65]]),
                            ploc[:, co_ + c0_: co_ + c1_],
                            start=first_, stop=last_)))(h, kb_par, j, c0, c1, colof,
                                                        pi == 0 and k2 == 0,
                                                        pi == npair - 1 and k2 == 1, oa_h),
                            waits=[ev_ex, ev_vlo, (r_nmL.get((b, h - 2)) if (pi == 0 and k2 == 0) else None)],
                            inc=(sE if (pi == npair - 1 and k2 == 1) else None),
                            tag=f"{tg}pv{b}_{h}_{pi}_{k2}")
                    r_pvL.set((b, h, pi), ev_pv)
                pendL[0] = (b, h, ev_pv, oa_h)
            b_flush_norm()
            for m in range(0 if (_BQONLY or _BNOWO) else NCH):
                bank = ps_b if m % 2 == 0 else ps_c
                evp = None
                for k in range(4):
                    evp = bld.emit("pe", (lambda bank_, k_, m_: (lambda e: e.matmul(
                        bank_[:], wo_sb[:, k_ * 1024 + m_ * 128: k_ * 1024 + m_ * 128 + 128],
                        oloc[:, k_ * 512:(k_ + 1) * 512],
                        start=(k_ == 0), stop=(k_ == 3))))(bank, k, m),
                        waits=[ev_wo, r_nmL.get((b, 2 * k + 1)), r_wo.get((b, m - 2))],
                        inc=(sE if k == 3 else None), tag=f"{tg}wo{b}_{m}")
                ev_add = bld.emit("dve", (lambda bank_, m_, t_: (lambda e: e.tensor_add(
                    st_ap(m_, t_, 512), st_ap(m_, t_, 512), bank_[:])))(bank, m, t0b),
                    waits=[evp], inc=sV, tag=f"{tg}woa{b}_{m}")
                r_wo.set((b, m), ev_add)
            ev_blk = bld.emit("dve", lambda e: e.engine_nop(),
                              waits=[r_wo.get((b, m2)) for m2 in range(NCH)] +
                                    ([r_qkev.get((b, 3))] if _BQONLY else []) +
                                    ([r_nmL.get((b, 7))] if _BNOWO and not _BQONLY else []),
                              inc=sV, tag=f"{tg}blk{b}")
            r_qkv.set((b + 1, -1), ev_blk)
        bld.barrier(SEMS, tag=f"{tg}attn_end")
        if stage == "B0a" and lyr == 0:
            return finish_dump_stream()

        ev_bnd2 = bld.emit("dve", lambda e: e.tensor_copy(
            AP(bound, 0, [[64, 128], [7, 8], [1, 7]]),
            AP(stream, 1 + 511, [[8 * CST, 128], [CST, 8], [512, 7]])), inc=sV, tag=tg + "bnd2")
        r_w1 = Ring(); r_ag = Ring(); r_w2 = Ring(); r_sttM = Ring(); r_statM = Ring(); r_muM = Ring()
        r_w1m = Ring()
        r_appM = Ring()
        for sidx in range(NSTRIP):
            t0 = sidx * 512
            ev_h2 = None
            for c in range(NCH):
                if c < 4:
                    ev_h2 = bld.emit("dve", (lambda c_, t_: (lambda e: e.tensor_copy(
                        h2bf[:, c_ * 512:(c_ + 1) * 512], st_ap(c_, t_, 512))))(c, t0),
                        waits=[ev_bnd2, r_w1.get(sidx * NC2 - 1)], tag=f"{tg}h2{sidx}_{c}")
                elif sidx == 0:
                    ev_h2 = bld.emit("dve", (lambda c_, t_: (lambda e: e.tensor_copy(
                        h2bf[:, c_ * 512:(c_ + 1) * 512], st_ap(c_, t_ - 1, 512))))(c, t0),
                        tag=f"{tg}h2{sidx}_{c}")
                else:
                    bld.emit("dve", (lambda c_, s_: (lambda e: e.tensor_copy(
                        h2bf[:, c_ * 512: c_ * 512 + 1], bound[:, c_ * 7 + s_ - 1: c_ * 7 + s_])))(c, sidx),
                        tag=f"{tg}h2b{sidx}_{c}")
                    ev_h2 = bld.emit("dve", (lambda c_, t_: (lambda e: e.tensor_copy(
                        h2bf[:, c_ * 512 + 1:(c_ + 1) * 512], st_ap(c_, t_, 511))))(c, t0),
                        tag=f"{tg}h2{sidx}_{c}")
            ev_h2 = bld.emit("dve", lambda e: e.engine_nop(), inc=sV, tag=f"{tg}h2f{sidx}")
            evm2 = None
            for c in range(NCH):
                csl = c % 2
                ev_sq = bld.emit("act", (lambda c_, csl_: (lambda e: e.activation(
                    sqb2[:, csl_ * 512:(csl_ + 1) * 512], h2bf[:, c_ * 512:(c_ + 1) * 512],
                    AF.Square, bias=CC.col(0.0), scale=CC.col(1.0))))(c, csl),
                    waits=[ev_h2, r_statM.get((sidx, c - 2))], inc=sA, tag=f"{tg}sq{sidx}_{c}")
                bld.emit("pe", (lambda c_: (lambda e: e.matmul(
                    ps_d[0:1, :], ones_bf[:, 0:1], h2bf[:, c_ * 512:(c_ + 1) * 512],
                    start=(c_ == 0), stop=(c_ == 7))))(c),
                    waits=[ev_sq, r_muM.get(sidx - 1)], tag=f"{tg}s1{sidx}_{c}")
                evm2 = bld.emit("pe", (lambda c_, csl_: (lambda e: e.matmul(
                    ps_a[0:1, 0:512], ones_bf[:, 0:1], sqb2[:, csl_ * 512:(csl_ + 1) * 512],
                    start=(c_ == 0), stop=(c_ == 7))))(c, csl), inc=sE, tag=f"{tg}s2{sidx}_{c}")
                r_statM.set((sidx, c), evm2)
            ev_bcL = emit_ln_rows(ps_d[0:1, :], ps_a[0:1, 0:512], [evm2, r_appM.get(sidx - 1)],
                                  f"{tg}lr{sidx}")
            r_muM.set(sidx, emit_ln_rows.last_var)
            ev_ap = None
            for c in range(NCH):
                bld.emit("dve", (lambda c_: (lambda e: e.tensor_mul(
                    qt1[:], h2bf[:, c_ * 512:(c_ + 1) * 512], ps_a[:, 512:1024])))(c),
                    waits=[ev_bcL], tag=f"{tg}ap1{sidx}_{c}")
                bld.emit("dve", lambda e: e.tensor_add(qt1[:], qt1[:], ps_a[:, 1024:1536]),
                         tag=f"{tg}ap2{sidx}_{c}")
                ev_ap = bld.emit("dve", (lambda c_: (lambda e: e.tensor_scalar(
                    h2bf[:, c_ * 512:(c_ + 1) * 512], qt1[:], lcols[:, c_:c_ + 1],
                    lcols[:, 8 + c_:8 + c_ + 1], ALU.mult, ALU.add)))(c),
                    inc=sV, tag=f"{tg}ap3{sidx}_{c}")
            r_appM.set(sidx, ev_ap)
            for cc in range(NC2):
                wsl = cc % 2
                ev_w1h = []
                for hf in range(2):
                    ev_w1h.append(bld.dma("pool", (sW1a if hf == 0 else sW1b)[wsl],
                                  w1buf[:, wsl * 2048 + hf * 1024: wsl * 2048 + hf * 1024 + 1024],
                                  AP(wscr, W1_OFF + (lyr * NC1 + 2 * cc) * 8 * 128 * 128 + hf * 8 * 16384,
                                     [[128, 128], [16384, 8], [1, 128]]),
                                  waits=[r_w1m.get(sidx * NC2 + cc - 2), ev_pc],
                                  tag=f"{tg}w1{sidx}_{cc}_{hf}"))
                ags = []
                for hf, (ag, off) in enumerate((("a", 0), ("g", 8))):
                    sl = (0 if ag == "a" else 2) + cc % 2
                    evp = None
                    for k in range(NCH):
                        evp = bld.emit("pe", (lambda sl_, k_, off_, wsl_: (lambda e: e.matmul(
                            ps_a[:, sl_ * 512:(sl_ + 1) * 512],
                            w1buf[:, wsl_ * 2048 + (off_ + k_) * 128: wsl_ * 2048 + (off_ + k_) * 128 + 128],
                            h2bf[:, k_ * 512:(k_ + 1) * 512],
                            start=(k_ == 0), stop=(k_ == NCH - 1))))(sl, k, off, wsl),
                            waits=[ev_w1h[hf], ev_ap, r_ag.get((sidx, cc - 2, ag))],
                            inc=(sE if k == NCH - 1 else None), tag=f"{tg}m1{sidx}_{cc}{ag}")
                    ags.append(evp)
                r_w1m.set(sidx * NC2 + cc, ags[1])
                ev_a = bld.emit("dve", (lambda sl_, cc_: (lambda e: e.tensor_scalar(
                    a_st[:], ps_a[:, sl_ * 512:(sl_ + 1) * 512], lcols[:, 16 + cc_:16 + cc_ + 1],
                    None, ALU.add)))(0 + cc % 2, cc), waits=[ags[0]], tag=f"{tg}ae{sidx}_{cc}")
                ev_g = bld.emit("act", (lambda sl_, cc_: (lambda e: e.activation(
                    g_st[:], ps_a[:, sl_ * 512:(sl_ + 1) * 512], AF.Gelu,
                    bias=lcols[:, 38 + cc_:38 + cc_ + 1], scale=CC.col(1.0))))(2 + cc % 2, cc),
                    waits=[ags[1], r_w1.get(sidx * NC2 + cc - 1)], inc=sA, tag=f"{tg}ge{sidx}_{cc}")
                ev_act = bld.emit("dve", (lambda cc_: (lambda e: e.tensor_mul(
                    act_st[:, cc_ * 512:(cc_ + 1) * 512], a_st[:], g_st[:])))(cc),
                    waits=[ev_a, ev_g], inc=sV, tag=f"{tg}am{sidx}_{cc}")
                r_ag.set((sidx, cc, "a"), ev_act)
                r_ag.set((sidx, cc, "g"), ev_act)
                r_w1.set(sidx * NC2 + cc, ev_act)
            for m in range(NCH):
                wsl = m % 2
                ev_w2h = []
                for hf, (t0h, nth) in enumerate(((0, 11), (11, 11))):
                    ev_w2h.append(bld.dma("pool", (sW2a if hf == 0 else sW2b)[wsl],
                                  w2buf[:, wsl * 2816 + t0h * 128: wsl * 2816 + (t0h + nth) * 128],
                                  AP(wscr, W2_OFF + (lyr * 8 + m) * NC2 * 128 * 128 + t0h * 16384,
                                     [[128, 128], [16384, nth], [1, 128]]),
                                  waits=[r_w2.get(sidx * 8 + m - 2), ev_pc],
                                  tag=f"{tg}w2{sidx}_{m}_{hf}"))
                bank = ps_b if m % 2 == 0 else ps_c
                evp = None
                for c in range(NC2):
                    evp = bld.emit("pe", (lambda bank_, c_, wsl_: (lambda e: e.matmul(
                        bank_[:], w2buf[:, wsl_ * 2816 + c_ * 128: wsl_ * 2816 + c_ * 128 + 128],
                        act_st[:, c_ * 512:(c_ + 1) * 512],
                        start=(c_ == 0), stop=(c_ == NC2 - 1))))(bank, c, wsl),
                        waits=[ev_w2h[0 if c < 11 else 1], r_w1.get(sidx * NC2 + c),
                               r_sttM.get((sidx, m - 2))],
                        inc=(sE if c == NC2 - 1 else None), tag=f"{tg}m2{sidx}_{m}")
                ev_stt = bld.emit("dve", (lambda bank_, m_, t_: (lambda e: e.scalar_tensor_tensor(
                    st_ap(m_, t_, 512), bank_[:], lcols[:, 60 + m_:60 + m_ + 1],
                    st_ap(m_, t_, 512), ALU.add, ALU.add)))(bank, m, t0),
                    waits=[evp], inc=sV, tag=f"{tg}w2a{sidx}_{m}")
                r_sttM.set((sidx, m), ev_stt)
                r_w2.set(sidx * 8 + m, evp)
            if _dmerge and lyr == 1:
                for c in range(NCH):
                    xb = x0_buf if c % 2 == 0 else x1b
                    ev_x1 = bld.dma("sp", sD0 if c % 2 == 0 else sD1, xb[:],
                                    AP(x1d, c * 128 * T + t0, [[T, 128], [1, 512]]),
                                    waits=[r_x1.get((sidx * 8 + c - 2))], tag=f"dx1{sidx}_{c}")
                    import os as _osm
                    _wser = ([r_sttM.get((sidx, mm_)) for mm_ in range(NCH)]
                             if _osm.environ.get("K_DM_SER", "0") == "1" else [r_sttM.get((sidx, c))])
                    ev_s1 = bld.emit("dve", (lambda c_, t_, xb_: (lambda e: e.tensor_add(
                        st_ap(c_, t_, 512), st_ap(c_, t_, 512), xb_[:])))(c, t0, xb),
                        waits=[ev_x1] + _wser, inc=sV, tag=f"ds1{sidx}_{c}")
                    r_x1.set(sidx * 8 + c, ev_s1)
        if not (_dmerge and lyr == 1) or _os0.environ.get("K_DM_BAR", "0") == "1":
            bld.barrier(SEMS, tag=f"{tg}mlp_end")
        if stage == "B0" and lyr == 0:
            return finish_dump_stream()
    if stage == "B1":
        return finish_dump_stream()

    _DNOLN = int(_os0.environ.get("K_D_NOLN", "0"))
    if _DNOLN == 5:
        ev_o = None
        for c in range(8):
            ev_o = bld.dma("sp", sS, AP(out_fm, c * 128 * T, [[T, 128], [1, T]]),
                           AP(x1d, c * 128 * T, [[T, 128], [1, T]]), tag=f"x1dmp{c}")
        bld.wait_only("sp", [ev_o], tag="final")
        return finish(nc)
    r_statD = Ring(); r_muD = Ring(); r_appD = Ring()
    ev_outs = []
    for sidx in range(NSTRIP):
        t0 = sidx * 512
        ev_s1 = None
        for c in range(NCH):
            if _DNOLN == 2 or _dmerge:
                break
            xb = x0_buf if c % 2 == 0 else (qt1 if _os0.environ.get("K_DM_QT1", "0") == "1" else q32)
            ev_x1 = bld.dma("sp", sD0 if c % 2 == 0 else sD1, xb[:],
                            AP(x1d, c * 128 * T + t0, [[T, 128], [1, 512]]),
                            waits=[r_x1.get((sidx * 8 + c - 2))], tag=f"dx1{sidx}_{c}")
            if _DNOLN == 3:
                ev_s1 = bld.emit("dve", (lambda c_, t_, xb_: (lambda e: e.tensor_copy(
                    st_ap(c_, t_, 512), xb_[:])))(c, t0, xb),
                    waits=[ev_x1], inc=sV, tag=f"ds1{sidx}_{c}")
            else:
                ev_s1 = bld.emit("dve", (lambda c_, t_, xb_: (lambda e: e.tensor_add(
                    st_ap(c_, t_, 512), st_ap(c_, t_, 512), xb_[:])))(c, t0, xb),
                    waits=[ev_x1], inc=sV, tag=f"ds1{sidx}_{c}")
            r_x1.set((sidx * 8 + c), ev_s1)
        if _DNOLN:
            ev_o = bld.dma("sp", sS, AP(out_fm, t0, [[T, 128], [128 * T, 8], [1, 512]]),
                           AP(stream, 1 + t0, [[8 * CST, 128], [CST, 8], [1, 512]]),
                           waits=[ev_s1], tag=f"doutn{sidx}")
            ev_outs.append(ev_o)
            r_appD.set(sidx, ev_s1)
            continue
        evm2 = None
        for c in range(NCH):
            csl = c % 2
            bld.emit("act", (lambda c_, csl_, t_: (lambda e: e.activation(
                s1bf[:, csl_ * 512:(csl_ + 1) * 512], st_ap(c_, t_, 512),
                AF.Copy, bias=0.0, scale=1.0)))(c, csl, t0),
                waits=[r_x1.get(sidx * 8 + c), r_statD.get((sidx, c - 2)),
                       (r_x1.get(NSTRIP * 8 - 1) if _dmerge else None)], tag=f"dsb{sidx}_{c}")
            ev_sqd = bld.emit("act", (lambda c_, csl_, t_: (lambda e: e.activation(
                sqbf[:, csl_ * 512:(csl_ + 1) * 512], st_ap(c_, t_, 512),
                AF.Square, bias=CC.col(0.0), scale=CC.col(1.0))))(c, csl, t0),
                inc=sA, tag=f"dsq{sidx}_{c}")
            xw_ = ([r_w1.get(NSTRIP * NC2 - 1), r_appM.get(NSTRIP - 1),
                    r_muM.get(NSTRIP - 1)] if (_dmerge and sidx == 0 and c == 0) else [])
            bld.emit("pe", (lambda c_, csl_: (lambda e: e.matmul(
                ps_d[0:1, :], ones_bf[:, 0:1], s1bf[:, csl_ * 512:(csl_ + 1) * 512],
                start=(c_ == 0), stop=(c_ == 7))))(c, csl),
                waits=[ev_sqd, r_muD.get(sidx - 1)] + xw_, tag=f"dm1{sidx}_{c}")
            evm2 = bld.emit("pe", (lambda c_, csl_: (lambda e: e.matmul(
                ps_a[0:1, 0:512], ones_bf[:, 0:1], sqbf[:, csl_ * 512:(csl_ + 1) * 512],
                start=(c_ == 0), stop=(c_ == 7))))(c, csl), inc=sE, tag=f"dm2{sidx}_{c}")
            r_statD.set((sidx, c), evm2)
        ev_bcD = emit_ln_rows(ps_d[0:1, :], ps_a[0:1, 0:512], [evm2, r_appD.get(sidx - 1)], f"dl{sidx}")
        r_muD.set(sidx, emit_ln_rows.last_var)
        ev_app = None
        for c in range(NCH):
            bld.emit("dve", (lambda c_, t_: (lambda e: e.tensor_mul(
                st_ap(c_, t_, 512), st_ap(c_, t_, 512), ps_a[:, 512:1024])))(c, t0),
                waits=[ev_bcD], tag=f"dap1{sidx}_{c}")
            ev_app = bld.emit("dve", (lambda c_, t_: (lambda e: e.tensor_add(
                st_ap(c_, t_, 512), st_ap(c_, t_, 512), ps_a[:, 1024:1536])))(c, t0),
                inc=sV, tag=f"dap2{sidx}_{c}")
        r_appD.set(sidx, ev_app)
        ev_o = bld.dma("sp", sS, AP(out_fm, t0, [[T, 128], [128 * T, 8], [1, 512]]),
                       AP(stream, 1 + t0, [[8 * CST, 128], [CST, 8], [1, 512]]),
                       waits=[ev_app], tag=f"dout{sidx}")
        ev_outs.append(ev_o)
    bld.wait_only("sp", [ev_outs[-1]], tag="final")
    return finish(nc)



def kernel(**inputs):
    import os
    ncores = int(os.environ.get("K_NCORES", "8"))
    shared, xs = _host_prep(inputs)
    nc = build_program(STAGE)
    in_maps = [dict(shared, x_fm=xs[i]) for i in range(ncores)]
    res = run_bass_kernel_spmd(nc, in_maps, core_ids=list(range(ncores)))
    out = np.stack([np.asarray(r["out_fm"]).T for r in res.results], axis=0)
    if ncores < 8:
        out = np.concatenate([out] + [out[:1]] * (8 - ncores), axis=0)
    return np.ascontiguousarray(out)


# version 32
# speedup vs baseline: 1.0011x; 1.0011x over previous
import numpy as np
import concourse.bass as bass
import concourse.mybir as mybir
from concourse.bass_utils import run_bass_kernel_spmd

F32 = mybir.dt.float32
BF16 = mybir.dt.bfloat16
AF = mybir.ActivationFunctionType
ALU = mybir.AluOpType
AX = mybir.AxisListType
AP = bass.AP

STAGE = "full"

T = 4096
NCH = 8
CST = 4104
NSTRIP = 8
SW = 512
HO, DO = 16, 64
HL, DL, WIN = 8, 64, 512
C_OUTER = 16.0
C_LOCAL = 8.0
HID = 2730
HIDP = 2816
NC2 = 22
NC1 = 44

SBASE = 16640
OFF_A = SBASE + 0
OFF_STREAM = SBASE + 65664
TAIL = SBASE + 197888
TAIL2 = TAIL + 10240
TAIL3 = TAIL2 + 2048
TAIL4 = TAIL2 + 4096

ENGINES = ("pe", "act", "dve", "pool", "sp")


class Sem:
    def __init__(self, handle, producer, name, is_dma=False):
        self.h, self.producer, self.name, self.count = handle, producer, name, 0
        self.is_dma = is_dma


class Builder:
    def __init__(self, nc):
        self.nc = nc
        self.steps = {e: [] for e in ENGINES}
        self.sems = []
        self.dma_sems = []
        self.waited = {e: {} for e in ENGINES}
        self.nwaits = 0

    def sem(self, name, producer, is_dma=False):
        h = self.nc.semaphore(name)
        s = Sem(h.__enter__(), producer, name, is_dma)
        self.sems.append((s, h))
        if is_dma:
            self.dma_sems.append(s)
        return s

    def emit(self, eng, fn, waits=(), inc=None, inc_amt=None, tag=""):
        w = []
        for ev in waits:
            if ev is None:
                continue
            s, v = ev
            if s.producer == eng and not s.is_dma:
                continue
            if v <= self.waited[eng].get(s.name, -1):
                continue
            self.waited[eng][s.name] = v
            w.append((s, v))
        self.nwaits += len(w)
        out_ev = None
        amt = 0
        if inc is not None:
            amt = inc_amt if inc_amt is not None else (16 if inc.producer in ("sp", "pool") else 1)
            inc.count += amt
            out_ev = (inc, inc.count)
        self.steps[eng].append((w, fn, inc, amt, tag))
        return out_ev

    def dma(self, q, sem, out_ap, in_ap, waits=(), tag="", **kw):
        return self.emit(q, (lambda o, i, kw2: (lambda e: e.dma_start(o, i, **kw2)))(out_ap, in_ap, kw),
                         waits=waits, inc=sem, tag=tag)

    def wait_only(self, eng, waits, tag=""):
        self.emit(eng, None, waits=waits, tag=tag)

    def barrier(self, sems_by_eng, tag="bar"):
        for e in ("sp", "pool"):
            self.wait_only(e, [(sm, sm.count) for sm in self.dma_sems
                              if sm.producer == e and sm.count], tag=tag + "_q")
        evs = []
        for e in ENGINES:
            evs.append(self.emit(e, lambda en: en.drain(), inc=sems_by_eng[e], inc_amt=1, tag=tag))
        for e in ENGINES:
            self.wait_only(e, evs, tag=tag + "_w")
        return evs

    def replay(self, block):
        def body(engname):
            def run(engobj):
                for (w, fn, inc, amt, tag) in self.steps[engname]:
                    for (s, v) in w:
                        engobj.wait_ge(s.h, v)
                    if fn is not None:
                        inst = fn(engobj)
                        if inc is not None:
                            inst.then_inc(inc.h, amt)
            return run
        block.tensor(body("pe"))
        block.scalar(body("act"))
        block.vector(body("dve"))
        block.gpsimd(body("pool"))
        block.sync(body("sp"))

    def close(self):
        for (_, h) in self.sems:
            h.__exit__(None, None, None)


class Ring:
    def __init__(self):
        self.d = {}

    def get(self, i):
        return self.d.get(i)

    def set(self, i, ev):
        self.d[i] = ev


class Consts:
    def __init__(self, t):
        self.t = t
        self.vals = {}

    def col(self, v, p0=0, np_=128):
        i = self.vals[float(v)]
        return self.t[p0:p0 + np_, i:i + 1]

    def reg(self, *vs):
        for v in vs:
            v = float(v)
            if v not in self.vals:
                self.vals[v] = len(self.vals)
        assert len(self.vals) <= self.t.shape[1]

    def emit_init(self, bld, inc):
        ev = None
        items = list(self.vals.items())
        for j, (v, i) in enumerate(items):
            ev = bld.emit("dve", (lambda vv, ii: (lambda e: e.memset(self.t[:, ii:ii + 1], vv)))(v, i),
                          inc=inc if j == len(items) - 1 else None, tag="const")
        return ev


def _host_prep(z):
    f32c = lambda a: np.ascontiguousarray(np.asarray(a, dtype=np.float32))
    sh = {}
    x = f32c(z["L0_attn_x"])
    xs = [np.ascontiguousarray(x[i].T) for i in range(8)]

    sh["wv_t"] = f32c(z["L0_attn_wv_w"].T)
    sh["wv_b"] = f32c(z["L0_attn_wv_b"]).reshape(1, 1024)
    w = f32c(z["L0_attn_w_att_val"])
    sh["watt"] = np.ascontiguousarray(
        np.concatenate([w, w], axis=2).transpose(1, 0, 2).reshape(64, 16 * 128))
    sh["outw_t"] = f32c(z["L0_attn_out_w"].T)
    sh["outb_col"] = np.ascontiguousarray(f32c(z["L0_attn_out_b"]).reshape(8, 128).T)

    sh["wqkv"] = f32c(z["L0_ffn_wqkv"])
    sh["wo"] = f32c(z["L0_ffn_wo"])
    sh["qs_t"] = np.ascontiguousarray(np.broadcast_to(f32c(z["L0_ffn_q_scale"])[:, None, :], (2, 128, 64)))
    sh["ks_t"] = np.ascontiguousarray(np.broadcast_to(f32c(z["L0_ffn_k_scale"])[:, None, :], (2, 128, 64)))
    sh["lng_col"] = np.ascontiguousarray(f32c(z["L0_ffn_ln_g"]).reshape(2, 8, 128).transpose(0, 2, 1))
    sh["lnb_col"] = np.ascontiguousarray(f32c(z["L0_ffn_ln_b"]).reshape(2, 8, 128).transpose(0, 2, 1))

    w1 = f32c(z["L0_ffn_w1"])
    b1 = f32c(z["L0_ffn_b1"])
    w1p = np.zeros((2, 1024, 2 * HIDP), np.float32)
    b1a = np.zeros((2, HIDP), np.float32)
    b1g = np.zeros((2, HIDP), np.float32)
    for c in range(NC2):
        lo, hi = c * 128, min((c + 1) * 128, HID)
        n = hi - lo
        w1p[:, :, (2 * c) * 128:(2 * c) * 128 + n] = w1[:, :, lo:hi]
        w1p[:, :, (2 * c + 1) * 128:(2 * c + 1) * 128 + n] = w1[:, :, HID + lo:HID + hi]
        b1a[:, c * 128:c * 128 + n] = b1[:, lo:hi]
        b1g[:, c * 128:c * 128 + n] = b1[:, HID + lo:HID + hi]
    sh["w1t"] = np.ascontiguousarray(
        w1p.reshape(2, 8, 128, NC1, 128).transpose(0, 3, 1, 2, 4))
    sh["b1a_col"] = np.ascontiguousarray(b1a.reshape(2, NC2, 128).transpose(0, 2, 1))
    sh["b1g_col"] = np.ascontiguousarray(b1g.reshape(2, NC2, 128).transpose(0, 2, 1))

    w2 = f32c(z["L0_ffn_w2"])
    w2p = np.zeros((2, HIDP, 1024), np.float32)
    w2p[:, :HID, :] = w2
    sh["w2t"] = np.ascontiguousarray(
        w2p.reshape(2, NC2, 128, 8, 128).transpose(0, 3, 1, 2, 4))
    sh["b2_col"] = np.ascontiguousarray(f32c(z["L0_ffn_b2"]).reshape(2, 8, 128).transpose(0, 2, 1))

    half = np.arange(0, DL, 2, dtype=np.float32)
    inv = (1.0 / 10000.0 ** (half / DL)).astype(np.float32)
    t = np.arange(T, dtype=np.float32)
    fr = (t[:, None] * inv[None, :]).astype(np.float32)
    fr = np.concatenate([fr, fr], 1)
    base = ((half + 0.4 * DL) / (1.4 * DL)).astype(np.float32)
    power = ((t - T // 2) / float(WIN // 2)).astype(np.float32)
    sc = (base[None, :] ** power[:, None]).astype(np.float32)
    sc = np.concatenate([sc, sc], 1)
    cos = np.cos(fr).astype(np.float32)
    sin = np.sin(fr).astype(np.float32)
    sgn = np.concatenate([-np.ones(32, np.float32), np.ones(32, np.float32)])
    sh["rot_csq"] = np.ascontiguousarray(cos * sc)
    sh["rot_ssq"] = np.ascontiguousarray(sin * sc * sgn)
    sh["rot_csk"] = np.ascontiguousarray(cos / sc)
    sh["rot_ssk"] = np.ascontiguousarray(sin / sc * sgn)

    tri = np.triu(np.ones((128, 128), np.float32))
    sh["masks"] = np.ascontiguousarray(np.stack(
        [-50.0 * (1.0 - tri.T).T, -50.0 * (1.0 - tri).T], 0))
    sh["ident"] = np.eye(128, dtype=np.float32)
    return sh, xs


def build_program(stage):
    nc = bass.Bass("TRN2", target_bir_lowering=False, detect_race_conditions=False)

    D = {}
    def din(name, shape):
        D[name] = nc.dram_tensor(name, list(shape), F32, kind="ExternalInput")
    din("x_fm", [1024, T])
    din("wv_t", [1024, 1024]); din("wv_b", [1, 1024])
    din("watt", [64, 16 * 128]); din("outw_t", [1024, 1024]); din("outb_col", [128, 8])
    din("wqkv", [2, 1024, 1536]); din("wo", [2, 512, 1024])
    din("qs_t", [2, 128, 64]); din("ks_t", [2, 128, 64])
    din("lng_col", [2, 128, 8]); din("lnb_col", [2, 128, 8])
    din("w1t", [2, NC1, 8, 128, 128]); din("b1a_col", [2, 128, NC2]); din("b1g_col", [2, 128, NC2])
    din("w2t", [2, 8, NC2, 128, 128]); din("b2_col", [2, 128, 8])
    din("rot_csq", [T, 64]); din("rot_ssq", [T, 64])
    din("rot_csk", [T, 64]); din("rot_ssk", [T, 64])
    din("masks", [2, 128, 128]); din("ident", [128, 128])
    x1d = nc.dram_tensor("x1d", [1025, T], F32, kind="Internal")
    OUTW_OFF, W1_OFF, W2_OFF = 0, 1024 * 1024, 1024 * 1024 + 2 * 44 * 8 * 128 * 128
    NW_SCR = W2_OFF + 2 * 8 * 22 * 128 * 128
    wscr = nc.dram_tensor("wscr", [NW_SCR], BF16, kind="Internal")
    out_fm = nc.dram_tensor("out_fm", [1024, T], F32, kind="ExternalOutput")

    import os as _osA
    _MIN = _osA.environ.get("K_MINALLOC", "0") == "1"
    _min_skip = {"stream", "o_buf", "outw_buf", "x0_buf", "s1bf", "sqbf", "rowb", "ident_sb"}
    def at(name, shape, dt, off):
        if _MIN and name in _min_skip:
            return None
        return nc.alloc_sbuf_tensor_at(name, shape, dt, offset=off)

    o_buf = at("o_buf", [128, 8 * 4096], BF16, OFF_A)
    wv_sb = at("wv_sb", [128, 8 * 1024], BF16, OFF_A)
    wvb32 = at("wvb32", [1, 1024], F32, OFF_A + 16384)
    wvbh32 = at("wvbh32", [1, 1024], F32, OFF_A + 20480)
    wvb2 = at("wvb2", [1, 2048], BF16, OFF_A + 24576)
    x_bf = at("x_bf", [128, 8 * CST], BF16, OFF_STREAM)
    vtm = at("vtm", [128, 32 * 1040], BF16, OFF_STREAM + 65664)
    stream = at("stream", [128, 8 * CST], F32, OFF_STREAM)

    watt_sb = at("watt_sb", [128, 16 * 128], BF16, TAIL + 0)
    xw_sb = at("xw_sb", [128, 2 * 512], BF16, TAIL + 4096)
    pq_buf = at("pq_buf", [128, 4 * 512], BF16, TAIL + 6144)
    outw_buf = at("outw_buf", [128, 2 * 8 * 128], BF16, TAIL + 0)
    x0_buf = at("x0_buf", [128, 512], F32, TAIL + 4096)
    s1bf = at("s1bf", [128, 2 * 512], BF16, TAIL + 6144)
    sqbf = at("sqbf", [128, 2 * 512], BF16, TAIL + 8192)
    wqkv_sb = at("wqkv_sb", [128, 8 * 1536], BF16, OFF_A + 0)
    wo_sb = at("wo_sb", [128, 4 * 1024], BF16, OFF_A + 24576)
    kfm = at("kfm", [128, 4 * 2 * 512], BF16, OFF_A + 32768)
    vloc = at("vloc", [128, 2 * 4 * 520], BF16, OFF_A + 40960)
    qfm = at("qfm", [128, 4 * 512], BF16, OFF_A + 49280)
    oloc = at("oloc", [128, 4 * 512], BF16, OFF_A + 53376)
    rotb = at("rotb", [128, 2 * 4 * 64], F32, OFF_A + 57472)
    ploc = at("ploc", [128, 4 * 512], BF16, OFF_A + 59520)
    lcols = at("lcols", [128, 80], F32, OFF_A + 63616)
    l2buf = at("l2buf", [128, 32], F32, OFF_A + 63936)
    bound = at("bound", [128, 64], F32, OFF_A + 64064)
    masks_sb = at("masks_sb", [128, 256], BF16, OFF_A + 64320)
    qsks = at("qsks", [128, 128], F32, OFF_A + 64832)
    act_st = at("act_st", [128, 22 * 512], BF16, OFF_A + 0)
    h2bf = at("h2bf", [128, 8 * 512], BF16, OFF_A + 22528)
    w1buf = at("w1buf", [128, 2 * 16 * 128], BF16, OFF_A + 30720)
    w2buf = at("w2buf", [128, 2 * 22 * 128], BF16, OFF_A + 38912)
    a_st = at("a_st", [128, 512], F32, OFF_A + 50176)
    g_st = at("g_st", [128, 512], F32, OFF_A + 52224)
    q32 = at("q32", [128, 512], F32, TAIL + 0)
    qfm_z = at("qfm_z", [128, 1024], BF16, TAIL + 0)
    qt1 = at("qt1", [128, 512], F32, TAIL + 2048)
    qt2 = at("qt2", [128, 512], F32, TAIL + 4096)
    qbf = at("qbf", [128, 512], BF16, TAIL + 6144)
    hbuf = at("hbuf", [128, 8 * 128], BF16, TAIL + 7168)
    x1b = at("x1b", [128, 512], F32, TAIL + 7168)
    sqb2 = at("sqb2", [128, 2 * 512], BF16, TAIL + 0)
    rowf = at("rowf", [128, 1024], F32, TAIL2)
    rowb = at("rowb", [128, 1024], BF16, TAIL3)
    cpool = at("cpool", [128, 16], F32, TAIL4)
    outb_sb = at("outb_sb", [128, 8], F32, TAIL4 + 64)
    ones_bf = at("ones_bf", [128, 128], BF16, TAIL4 + 96)
    ident_sb = at("ident_sb", [128, 128], BF16, OFF_A + 65344)

    pg = []
    def ps(name, shape, dt=F32):
        g = nc.psum_tensor(name, shape, dt)
        tt = g.__enter__(); pg.append(g); return tt
    ps_a = ps("ps_a", [128, 2048])
    ps_b = ps("ps_b", [128, 512])
    ps_c = ps("ps_c", [128, 512])
    ps_d = ps("ps_d", [128, 512])
    pt_t = ps("pt_t", [128, 1024], BF16)

    bld = Builder(nc)
    sP = bld.sem("sP", "pool", is_dma=True)
    sS = bld.sem("sS", "sp", is_dma=True)
    sE = bld.sem("sE", "pe")
    sA = bld.sem("sA", "act")
    sV = bld.sem("sV", "dve")
    sPb = bld.sem("sPb", "pool")
    sSb = bld.sem("sSb", "sp")
    sD0 = bld.sem("sD0", "sp", is_dma=True)
    sD1 = bld.sem("sD1", "sp", is_dma=True)
    sN1 = bld.sem("sN1", "sp", is_dma=True)
    sN2 = bld.sem("sN2", "sp", is_dma=True)
    sR = [bld.sem(f"sR{i}", "sp", is_dma=True) for i in range(8)]
    sX = [bld.sem(f"sX{i}", "pool", is_dma=True) for i in range(8)]
    sWV = bld.sem("sWV", "pool", is_dma=True)
    sWA = [bld.sem(f"sWA{i}", "pool", is_dma=True) for i in range(2)]
    sVB = bld.sem("sVB", "sp", is_dma=True)
    sOB = bld.sem("sOB", "sp", is_dma=True)
    sOW = [bld.sem(f"sOW{i}", "pool", is_dma=True) for i in range(2)]
    sX0 = bld.sem("sX0b", "sp", is_dma=True)
    sID = bld.sem("sID", "pool", is_dma=True)
    sMK = bld.sem("sMK", "pool", is_dma=True)
    sWQ = bld.sem("sWQ", "pool", is_dma=True)
    sWO = bld.sem("sWO", "pool", is_dma=True)
    sQS = bld.sem("sQS", "sp", is_dma=True)
    sKS = bld.sem("sKS", "sp", is_dma=True)
    sLC = [bld.sem(f"sLC{i}", "sp", is_dma=True) for i in range(6)]
    sW1a = [bld.sem(f"sW1a{i}", "pool", is_dma=True) for i in range(2)]
    sW1b = [bld.sem(f"sW1b{i}", "pool", is_dma=True) for i in range(2)]
    sW2a = [bld.sem(f"sW2a{i}", "pool", is_dma=True) for i in range(2)]
    sW2b = [bld.sem(f"sW2b{i}", "pool", is_dma=True) for i in range(2)]
    sPC = bld.sem("sPC", "pool", is_dma=True)
    SEMS = {"pe": sE, "act": sA, "dve": sV, "pool": sPb, "sp": sSb}
    bld.queue_sems = {"pool": sP, "sp": sS}

    import os as _os0
    _SKIP = set(_os0.environ.get("K_SKIP", "").split(","))
    CC = Consts(cpool)
    CC.reg(0.0, 1.0, -1.0, 0.125, -C_OUTER, -C_LOCAL, 1.0 / 1024.0, 1e-5, 8.0, 1e-24, -0.5,
           float(np.log(8.0)))
    if "consts" in _SKIP:
        ev_cc = bld.emit("dve", lambda e: e.memset(cpool[:], 0.0), inc=sV, tag="cz")
    else:
        ev_cc = CC.emit_init(bld, inc=sV)
    ev_ones = bld.emit("dve", lambda e: e.memset(ones_bf[:], 1.0), inc=sV, tag="ones")
    if "outb" in _SKIP:
        ev_outb = None
    else:
        ev_outb = bld.dma("sp", sOB, outb_sb[:], D["outb_col"].ap(), tag="outb")

    def finish(nc_):
        with nc_.Block() as blk:
            bld.replay(blk)
        bld.close()
        for g in pg[::-1]:
            g.__exit__(None, None, None)
        counts = {e: len(bld.steps[e]) for e in ENGINES}
        counts["waits"] = bld.nwaits
        print("[kernel] step counts:", counts)
        return nc_

    def finish_dump_bf16(src_ap):
        ev_o = bld.dma("pool", sP, AP(out_fm, 0, [[T, 128], [128 * T, 8], [1, 4096]]), src_ap, tag="dump")
        bld.wait_only("pool", [ev_o], tag="final")
        return finish(nc)

    def st_ap(c, t0, w):
        return stream[:, c * CST + 1 + t0: c * CST + 1 + t0 + w]

    def xbf_ap(c, t0, w, base=0, np_=128):
        return x_bf[base:base + np_, c * CST + 1 + t0: c * CST + 1 + t0 + w]

    ev_xbf = []
    _ULTRA = _os0.environ.get("K_ULTRA", "0") == "1"
    _U8 = _os0.environ.get("K_U8", "0") == "1"
    for c in range((8 if _U8 else 1) if _ULTRA else NCH):
        src = AP(D["x_fm"], c * 128 * T, [[T, 128], [1, T]])
        ev_xbf.append(bld.dma("pool", sX[c % 8], xbf_ap(c, 0, T), src, tag=f"xbf{c}"))
    if _ULTRA:
        _UADD = set(_os0.environ.get("K_UADD", "").split(","))
        ev_wvU = bld.dma("pool", sWV, wv_sb[:],
                         AP(D["wv_t"], 0, [[1024, 128], [128 * 1024, 8], [1, 1024]]), tag="wvU")
        if "watt" in _UADD:
            bld.dma("pool", sWA[0], watt_sb[0:64, :], D["watt"].ap(), tag="uwa0")
            bld.dma("pool", sWA[1], watt_sb[64:128, :], D["watt"].ap(), tag="uwa1")
        if "hilo" in _UADD:
            ev_wvbU = bld.dma("sp", sVB, wvb32[:], D["wv_b"].ap(), tag="uwvb")
            bld.emit("dve", lambda e: e.tensor_copy(wvb2[0:1, 0:1024], wvb32[:]), waits=[ev_wvbU], tag="uh1")
            bld.emit("dve", lambda e: e.tensor_copy(wvbh32[:], wvb2[0:1, 0:1024]), tag="uh2")
            bld.emit("dve", lambda e: e.tensor_sub(wvb2[0:1, 1024:2048], wvb32[:], wvbh32[:]), inc=sV, tag="uh3")
        if "vones" in _UADD:
            bld.emit("dve", lambda e: e.memset(AP(vtm, 64, [[32 * 1040, 128], [65, 512], [1, 1]]), 1.0),
                     inc=sV, tag="uvones")
        ev_m1 = bld.emit("pe", lambda e: e.matmul(ps_a[:, 0:512], x_bf[:, 1:129],
                                                  wv_sb[:, 0:512], start=True, stop=True),
                         waits=[ev_xbf[0], ev_wvU], inc=sE, tag="umm")
        ev_cp = bld.emit("dve", lambda e: e.tensor_copy(vtm[:, 0:512], ps_a[:, 0:512]),
                         waits=[ev_m1], inc=sV, tag="ucp")
        if "barq" in _UADD:
            for e in ("sp", "pool"):
                sm = bld.queue_sems[e]
                bld.wait_only(e, [(sm, sm.count)], tag="ubq")
        if "barn" in _UADD:
            for e in ENGINES:
                bld.emit(e, lambda en: en.nop(), inc=SEMS[e], inc_amt=1, tag="ubn")
        for e in ENGINES:
            if ("barn_" + e) in _UADD:
                bld.emit(e, lambda en: en.drain(), inc=SEMS[e], inc_amt=1, tag="ubn1")
        if "bar" in _UADD:
            bld.barrier(SEMS, tag="ubar")
        ev_o = bld.dma("pool", sP, AP(out_fm, 0, [[T, 128], [128 * T, 8], [1, 4096]]),
                       x_bf[:, 0:32768], waits=[ev_cp], tag="udump")
        bld.wait_only("pool", [ev_o], tag="final")
        return finish(nc)
    if stage == "Z0":
        ev_o = bld.dma("pool", sP, AP(out_fm, 0, [[T, 128], [128 * T, 8], [1, 4096]]),
                       x_bf[:, 0:32768], waits=ev_xbf, tag="dumpz")
        bld.wait_only("pool", [ev_o], tag="final")
        return finish(nc)
    ev_wv = bld.dma("pool", sWV, wv_sb[:],
                    AP(D["wv_t"], 0, [[1024, 128], [128 * 1024, 8], [1, 1024]]), tag="wv")
    if "watt" in _SKIP:
        ev_wa0 = ev_wa1 = None
    else:
        ev_wa0 = bld.dma("pool", sWA[0], watt_sb[0:64, :], D["watt"].ap(), tag="watt0")
        ev_wa1 = bld.dma("pool", sWA[1], watt_sb[64:128, :], D["watt"].ap(), tag="watt1")
    ev_wvb = bld.dma("sp", sVB, wvb32[:], D["wv_b"].ap(), tag="wvb")
    _PCH = 32768
    for (nm_, dst0_, n_) in (("outw_t", OUTW_OFF, 1024 * 1024),
                             ("w1t", W1_OFF, 2 * 44 * 8 * 128 * 128),
                             ("w2t", W2_OFF, 2 * 8 * 22 * 128 * 128)):
        for k_ in range(0, n_, _PCH):
            w_ = min(_PCH, n_ - k_)
            bld.dma("pool", sPC, AP(wscr, dst0_ + k_, [[1, w_]]),
                    AP(D[nm_], k_, [[1, w_]]), tag=f"pc_{nm_}_{k_ // _PCH}")
    ev_pc = (sPC, sPC.count)
    if "hilo" in _SKIP:
        bld.emit("dve", lambda e: e.memset(wvb2[0:1, :], 0.0), waits=[ev_wvb], tag="wvbh")
    else:
        bld.emit("dve", lambda e: e.tensor_copy(wvb2[0:1, 0:1024], wvb32[:]), waits=[ev_wvb], tag="wvbh")
    bld.emit("dve", lambda e: e.tensor_copy(wvbh32[:], wvb2[0:1, 0:1024]), tag="wvbh32")
    ev_h3 = bld.emit("dve", lambda e: e.tensor_sub(wvb2[0:1, 1024:2048], wvb32[:], wvbh32[:]), inc=sV, tag="wvblo")
    if "vones" in _SKIP:
        ev_vones = bld.emit("dve", lambda e: e.memset(vtm[:, 0:64], 1.0), inc=sV, tag="vones")
    else:
        ev_vones = bld.emit("dve", lambda e: e.memset(AP(vtm, 64, [[32 * 1040, 128], [65, 512], [1, 1]]), 1.0),
                            inc=sV, tag="vones")

    if stage == "Z1":
        import os as _os1
        if _os1.environ.get("K_Z1MM", "0") == "1":
            ev_m1 = bld.emit("pe", lambda e: e.matmul(ps_a[:, 0:512], x_bf[:, 1:129],
                                                      wv_sb[:, 0:512], start=True, stop=True),
                             waits=[ev_xbf[0], ev_wv], inc=sE, tag="zmm")
            bld.emit("dve", lambda e: e.tensor_copy(vtm[:, 0:512], ps_a[:, 0:512]),
                     waits=[ev_m1], inc=sV, tag="zcp")
        bld.barrier(SEMS, tag="z1")
        return finish_dump_bf16(x_bf[:, 0:32768])
    import os as _os
    _NT = int(_os.environ.get("K_VX_NT", "32"))
    _NOBIAS = _os.environ.get("K_VX_NOBIAS", "0") == "1"
    _NOEVAC = _os.environ.get("K_VX_NOEVAC", "0") == "1"
    r_evac = Ring()
    for ti in range(_NT):
        for eh in range(2):
            idx = 2 * ti + eh
            sl = idx % 4
            _NMM = int(_os.environ.get("K_VX_NMM", str(NCH)))
            for k in range(_NMM):
                w_ = [ev_xbf[k], ev_wv]
                if k == 0:
                    w_.append(r_evac.get(idx - 4))
                bld.emit("pe", (lambda sl_, k_, ti_, eh_: (lambda e: e.matmul(
                    ps_a[:, sl_ * 512:(sl_ + 1) * 512],
                    xbf_ap(k_, ti_ * 128, 128),
                    wv_sb[:, k_ * 1024 + eh_ * 512: k_ * 1024 + eh_ * 512 + 512],
                    start=(k_ == 0), stop=(_NOBIAS and k_ == _NMM - 1))))(sl, k, ti, eh),
                    waits=w_, tag=f"vx{idx}k{k}")
            if not _NOBIAS:
                bld.emit("pe", (lambda sl_, eh_: (lambda e: e.matmul(
                    ps_a[:, sl_ * 512:(sl_ + 1) * 512], ones_bf[0:1, 0:128],
                    wvb2[0:1, eh_ * 512: eh_ * 512 + 512], start=False, stop=False)))(sl, eh),
                    waits=[ev_h3, ev_ones], tag=f"vxbh{idx}")
                evm = bld.emit("pe", (lambda sl_, eh_: (lambda e: e.matmul(
                    ps_a[:, sl_ * 512:(sl_ + 1) * 512], ones_bf[0:1, 0:128],
                    wvb2[0:1, 1024 + eh_ * 512: 1024 + eh_ * 512 + 512], start=False, stop=True)))(sl, eh),
                    inc=sE, tag=f"vxbl{idx}")
            else:
                evm = bld.emit("pe", lambda e: e.nop(), waits=[ev_h3, ev_ones], inc=sE, tag=f"vxnop{idx}")
            if _NOEVAC:
                dst = vtm[:, (idx % 32) * 512:(idx % 32) * 512 + 512]
                ev = bld.emit("dve", (lambda sl_, dst_: (lambda e: e.tensor_copy(
                    dst_, ps_a[:, sl_ * 512:(sl_ + 1) * 512])))(sl, dst),
                    waits=[evm], inc=sV, tag=f"vxe{idx}")
            else:
                dst = AP(vtm, ti * 1040 + eh * 520, [[32 * 1040, 128], [65, 8], [1, 64]])
                ev = bld.emit("dve", (lambda sl_, dst_: (lambda e: e.tensor_copy(
                    dst_, ps_a[:, sl_ * 512:(sl_ + 1) * 512].rearrange("p (h d) -> p h d", d=64))))(sl, dst),
                    waits=[evm], inc=sV, tag=f"vxe{idx}")
            r_evac.set(idx, ev)
    ev_vx_done = ev

    if stage == "A0":
        bld.barrier(SEMS, tag="a0")
        return finish_dump_bf16(x_bf[:, 0:32768])
    if stage == "A1":
        bld.barrier(SEMS, tag="a1")
        ev_o = bld.dma("pool", sP, AP(out_fm, 0, [[32768, 128], [1024, 32], [1, 1024]]),
                       AP(vtm, 0, [[32 * 1040, 128], [1040, 32], [1, 1024]]), tag="dumpv")
        bld.wait_only("pool", [ev_o], tag="final")
        return finish(nc)

    _A2NIT = int(_os0.environ.get("K_A2_NIT", "128"))
    _A2NONORM = _os0.environ.get("K_A2_NONORM", "0") == "1"
    _A2NOPV = _os0.environ.get("K_A2_NOPV", "0") == "1"
    _A2NOEXP = _os0.environ.get("K_A2_NOEXP", "0") == "1"
    r_exp = Ring(); r_pv = Ring(); r_norm = Ring(); r_strd = Ring(); r_xwe = Ring(); r_bce = Ring()
    r_d1 = Ring()
    pend_norm = [None]
    ev_zx = bld.emit("dve", lambda e: e.memset(xw_sb[64:128, 0:512], 0.0), tag="zx0")
    ev_zx = bld.emit("dve", lambda e: e.memset(xw_sb[0:64, 512:1024], 0.0), inc=sV, tag="zx1")
    def a2_flush_norm():
        if pend_norm[0] is None:
            return
        it_, o_acc_, dst_, ev_pv_ = pend_norm[0]
        pend_norm[0] = None
        bld.emit("dve", (lambda o_: (lambda e: e.tensor_copy(rowf[0:1, 0:512], o_[64:65, :])))(o_acc_),
                 waits=[ev_pv_, r_d1.get(it_ - 1)], tag=f"nr1_{it_}")
        ev_rec = bld.emit("dve", lambda e: e.reciprocal(rowf[32:33, 0:512], rowf[0:1, 0:512]),
                          inc=sV, tag=f"nr2_{it_}")
        sl = it_ % 8
        ev_d1 = bld.dma("sp", sN1, AP(x1d, 1024 * T + sl * 512, [[512, 1], [1, 512]]), rowf[32:33, 0:512],
                        waits=[ev_rec], tag=f"nrd1_{it_}")
        r_d1.set(it_, ev_d1)
        ev_d2 = bld.dma("sp", sN2, rowf[0:64, 512:1024], AP(x1d, 1024 * T + sl * 512, [[0, 64], [1, 512]]),
                        waits=[ev_d1, r_bce.get(it_ - 1)], tag=f"nrd2_{it_}")
        ev_nm = bld.emit("dve", (lambda dst2, o_, evd: (lambda e: e.tensor_mul(
            dst2, o_[0:64, :], rowf[0:64, 512:1024])))(dst_, o_acc_, ev_d2),
            waits=[ev_d2], inc=sV, tag=f"norm_{it_}")
        r_bce.set(it_, ev_nm)
        r_norm.set(it_, ev_nm)
    it = 0
    ptf = pt_t[:].bitcast(F32)
    def emit_xw(it_):
        q_, h_ = divmod(it_, HO)
        b_ = 64 * (h_ % 2)
        ch_ = h_ // 2
        xsl_ = it_ % 2
        mm = bld.emit("pe", (lambda: (lambda e: e.matmul(
            ptf[:, 0:512], watt_sb[b_:b_ + 64, h_ * 128:(h_ + 1) * 128],
            xbf_ap(ch_, q_ * SW, SW, base=b_, np_=64), start=True, stop=True)))(),
            waits=[ev_wa0, ev_wa1, r_xwe.get(it_ - 1)], inc=sE, tag=f"xw{it_}")
        ev = bld.emit("dve", (lambda: (lambda e: e.tensor_copy(
            xw_sb[b_:b_ + 64, xsl_ * 512:(xsl_ + 1) * 512], ptf[b_:b_ + 64, 0:512])))(),
            waits=[mm, r_strd.get(it_ - 2), ev_vx_done, ev_zx], inc=sV, tag=f"xwe{it_}")
        r_xwe.set(it_, ev)
        return ev
    if _A2NIT > 0:
        emit_xw(0)
    for q in range(NSTRIP):
        if it >= _A2NIT:
            break
        for h in range(HO):
            if it >= _A2NIT:
                break
            bse = 64 * (h % 2)
            ch = h // 2
            o_acc = ps_b if it % 2 == 0 else ps_c
            xsl = it % 2
            ev_xw = r_xwe.get(it)
            ev_pv = None
            for p in range(17):
                if p == 1 and it + 1 < _A2NIT:
                    emit_xw(it + 1)
                if p < 16:
                    pi = it * 16 + p
                    for jj in range(2):
                        j = 2 * p + jj
                        sl = j % 4
                        ev_st = bld.emit("pe", (lambda b, ch_, j_, sl_, xsl_: (lambda e: e.matmul(
                            ps_a[:, sl_ * 512:(sl_ + 1) * 512],
                            xbf_ap(ch_, j_ * 128, 128, base=0, np_=128),
                            xw_sb[0:128, xsl_ * 512:(xsl_ + 1) * 512],
                            start=True, stop=True)))(bse, ch, j, sl, xsl),
                            waits=[ev_xw, r_exp.get(pi - 2)], inc=(sE if jj == 1 else None),
                            tag=f"st{it}_{j}")
                    r_strd.set(it, ev_st)
                    sl0 = (2 * p) % 4
                    if _A2NOEXP:
                        ev_exp = bld.emit("act", (lambda p_, sl0_: (lambda e: e.activation(
                            pq_buf[:, (p_ % 2) * 1024:(p_ % 2) * 1024 + 1024],
                            ps_a[:, sl0_ * 512: sl0_ * 512 + 1024],
                            AF.Copy, bias=CC.col(0.0), scale=CC.col(0.125))))(p, sl0),
                            waits=[ev_st, ev_cc, r_pv.get(pi - 2)], inc=sA, tag=f"expc{it}_{p}")
                    else:
                        ev_exp = bld.emit("act", (lambda p_, sl0_: (lambda e: e.activation(
                            pq_buf[:, (p_ % 2) * 1024:(p_ % 2) * 1024 + 1024],
                            ps_a[:, sl0_ * 512: sl0_ * 512 + 1024],
                            AF.Exp, bias=CC.col(-C_OUTER), scale=CC.col(0.125))))(p, sl0),
                            waits=[ev_st, ev_cc, r_pv.get(pi - 2)], inc=sA, tag=f"exp{it}_{p}")
                    r_exp.set(pi, ev_exp)
                if p >= 1 and not _A2NOPV:
                    pp = p - 1
                    ppi = it * 16 + pp
                    for jj in range(2):
                        j = 2 * pp + jj
                        w_ = [r_exp.get(ppi), ev_vones]
                        if j == 0:
                            w_.append(r_norm.get(it - 2))
                        ev_pv = bld.emit("pe", (lambda h_, j_, pp_, o_: (lambda e: e.matmul(
                            o_[0:65, :],
                            AP(vtm, j_ * 1040 + h_ * 65, [[32 * 1040, 128], [1, 65]]),
                            pq_buf[:, ((pp_ % 2) * 2 + (j_ % 2)) * 512: ((pp_ % 2) * 2 + (j_ % 2)) * 512 + 512],
                            start=(j_ == 0), stop=(j_ == 31))))(h, j, pp, o_acc),
                            waits=w_, inc=(sE if jj == 1 else None), tag=f"pv{it}_{j}")
                    r_pv.set(ppi, ev_pv)
            if _A2NOPV:
                ev_pv = bld.emit("pe", (lambda o_: (lambda e: e.matmul(
                    o_[0:65, :], AP(vtm, 0 * 1040, [[32 * 1040, 128], [1, 65]]),
                    pq_buf[:, 0:512], start=True, stop=True)))(o_acc),
                    waits=[r_exp.get(it * 16), r_norm.get(it - 2), ev_vones], inc=sE, tag=f"pv1_{it}")
            if _A2NONORM:
                dst0 = o_buf[bse:bse + 64, ch * 4096 + q * SW: ch * 4096 + q * SW + SW]
                ev_nm = bld.emit("dve", (lambda dst_, o_: (lambda e: e.tensor_copy(
                    dst_, o_[0:64, :])))(dst0, o_acc), waits=[ev_pv], inc=sV, tag=f"nmc_{it}")
                r_norm.set(it, ev_nm)
                it += 1
                continue
            a2_flush_norm()
            dst = o_buf[bse:bse + 64, ch * 4096 + q * SW: ch * 4096 + q * SW + SW]
            pend_norm[0] = (it, o_acc, dst, ev_pv)
            it += 1
    a2_flush_norm()
    bld.barrier(SEMS, tag="a2end")
    if stage == "A2":
        return finish_dump_bf16(o_buf[:, 0:32768])

    ev_z = bld.emit("dve", lambda e: e.memset(AP(stream, 0, [[8 * CST, 128], [CST, 8], [1, 1]]), 0.0),
                    inc=sV, tag="zcol")
    r_ow = Ring(); r_stt = Ring(); r_stat = Ring(); r_app = Ring(); r_mu = Ring()
    for s in range(NSTRIP):
        for m in range(NCH):
            im = s * 8 + m
            osl = m % 2
            src = AP(wscr, OUTW_OFF + m * 128, [[1024, 128], [128 * 1024, 8], [1, 128]])
            ev_ow = bld.dma("pool", sOW[osl], outw_buf[:, osl * 1024: osl * 1024 + 1024], src,
                            waits=[r_ow.get(im - 2), ev_pc], tag=f"ow{im}")
            bank = ps_b if m % 2 == 0 else ps_c
            evp = None
            for k in range(NCH):
                evp = bld.emit("pe", (lambda bank_, osl_, k_, s_: (lambda e: e.matmul(
                    bank_[:], outw_buf[:, osl_ * 1024 + k_ * 128: osl_ * 1024 + k_ * 128 + 128],
                    o_buf[:, k_ * 4096 + s_ * SW: k_ * 4096 + s_ * SW + SW],
                    start=(k_ == 0), stop=(k_ == 7))))(bank, osl, k, s),
                    waits=[ev_ow, r_stt.get(im - 2)], inc=(sE if k == 7 else None), tag=f"op{im}")
            r_ow.set(im, evp)
            ev_x0 = bld.dma("sp", sX0, x0_buf[:], D["x_fm"][m * 128:(m + 1) * 128, s * SW:(s + 1) * SW],
                            waits=[r_stt.get(im - 1)], tag=f"x0{im}")
            ev_stt = bld.emit("dve", (lambda bank_, m_, s_: (lambda e: e.scalar_tensor_tensor(
                st_ap(m_, s_ * SW, SW), bank_[:], outb_sb[:, m_:m_ + 1], x0_buf[:],
                ALU.add, ALU.add)))(bank, m, s),
                waits=[evp, ev_x0, ev_z, ev_outb], inc=sV, tag=f"stt{im}")
            r_stt.set(im, ev_stt)
        evm1 = evm2 = None
        for c in range(NCH):
            ic = s * 8 + c
            csl = c % 2
            bld.emit("act", (lambda c_, s_, csl_: (lambda e: e.activation(
                s1bf[:, csl_ * 512:(csl_ + 1) * 512], st_ap(c_, s_ * SW, SW),
                AF.Copy, bias=0.0, scale=1.0)))(c, s, csl),
                waits=[r_stt.get(s * 8 + c), r_stat.get(ic - 2)], tag=f"sb{ic}")
            ev_c2 = bld.emit("act", (lambda c_, s_, csl_: (lambda e: e.activation(
                sqbf[:, csl_ * 512:(csl_ + 1) * 512], st_ap(c_, s_ * SW, SW),
                AF.Square, bias=CC.col(0.0), scale=CC.col(1.0))))(c, s, csl),
                inc=sA, tag=f"sq{ic}")
            evm1 = bld.emit("pe", (lambda c_, csl_: (lambda e: e.matmul(
                ps_d[0:1, :], ones_bf[:, 0:1], s1bf[:, csl_ * 512:(csl_ + 1) * 512],
                start=(c_ == 0), stop=(c_ == 7))))(c, csl),
                waits=[ev_c2, r_mu.get(s - 1)], tag=f"sm1{ic}")
            evm2 = bld.emit("pe", (lambda c_, csl_: (lambda e: e.matmul(
                ps_a[0:1, 0:512], ones_bf[:, 0:1], sqbf[:, csl_ * 512:(csl_ + 1) * 512],
                start=(c_ == 0), stop=(c_ == 7))))(c, csl), inc=sE, tag=f"sm2{ic}")
            r_stat.set(ic, evm2)
        r0 = rowf[0:1, 0:512]; r32 = rowf[32:33, 0:512]
        r64 = rowf[64:65, 0:512]; r64b = rowf[64:65, 512:1024]
        bld.emit("dve", lambda e: e.tensor_scalar(r0, ps_d[0:1, :], CC.col(1.0 / 1024.0, 0, 1),
                                                  None, ALU.mult), waits=[evm2], tag=f"mu{s}")
        ev_mu = bld.emit("dve", lambda e: e.tensor_scalar(r32, ps_a[0:1, 0:512],
                                                          CC.col(1.0 / 1024.0, 0, 1), None, ALU.mult),
                         inc=sV, tag=f"e2{s}")
        r_mu.set(s, ev_mu)
        bld.emit("dve", lambda e: e.tensor_mul(r64, r0, r0), tag=f"mu2{s}")
        bld.emit("dve", lambda e: e.tensor_copy(r64b, r32), tag=f"e2c{s}")
        ev_var = bld.emit("dve", lambda e: e.tensor_sub(r32, r64b, r64), inc=sV, tag=f"var{s}")
        bld.emit("act", lambda e: e.activation(r64, r32, AF.Ln,
                                               bias=CC.col(1e-5, 32, 1), scale=CC.col(1.0, 32, 1)),
                 waits=[ev_var], tag=f"ln{s}")
        ev_rstd = bld.emit("act", lambda e: e.activation(r32, r64, AF.Exp,
                                                         bias=CC.col(0.0, 64, 1), scale=CC.col(-0.5, 64, 1)),
                           inc=sA, tag=f"rstd{s}")
        bld.emit("dve", lambda e: e.tensor_copy(r64, r32), waits=[ev_rstd], tag=f"rsc{s}")
        bld.emit("dve", lambda e: e.tensor_copy(r64b, r0), tag=f"muc{s}")
        bld.emit("dve", lambda e: e.tensor_mul(r0, r64, r64b), tag=f"r2a{s}")
        bld.emit("dve", lambda e: e.tensor_scalar(r0, r0, CC.col(-1.0, 0, 1), None, ALU.mult),
                 tag=f"r2b{s}")
        bld.emit("dve", lambda e: e.tensor_copy(rowb[0:1, 0:512], r32), tag=f"hl1{s}")
        bld.emit("dve", lambda e: e.tensor_copy(r64b, rowb[0:1, 0:512]), tag=f"hl2{s}")
        ev_hl = bld.emit("dve", lambda e: e.tensor_sub(rowb[0:1, 512:1024], r64, r64b),
                         inc=sV, tag=f"hl3{s}")
        bld.emit("pe", lambda e: e.matmul(ps_a[:, 512:1024], ones_bf[0:1, 0:128], rowb[0:1, 0:512],
                                          start=True, stop=False), waits=[ev_hl, r_app.get(s - 1)],
                 tag=f"rb1{s}")
        ev_rb2 = bld.emit("pe", lambda e: e.matmul(ps_a[:, 512:1024], ones_bf[0:1, 0:128],
                                                   rowb[0:1, 512:1024], start=False, stop=True),
                          inc=sE, tag=f"rb2{s}")
        bld.emit("dve", lambda e: e.tensor_copy(rowb[0:1, 0:512], r0), waits=[ev_rb2], tag=f"hl4{s}")
        bld.emit("dve", lambda e: e.tensor_copy(r64, rowb[0:1, 0:512]), tag=f"hl5{s}")
        bld.emit("dve", lambda e: e.tensor_copy(r64b, r0), tag=f"hl5b{s}")
        ev_hl2 = bld.emit("dve", lambda e: e.tensor_sub(rowb[0:1, 512:1024], r64b, r64),
                          inc=sV, tag=f"hl6{s}")
        bld.emit("pe", lambda e: e.matmul(ps_a[:, 1024:1536], ones_bf[0:1, 0:128], rowb[0:1, 0:512],
                                          start=True, stop=False), waits=[ev_hl2], tag=f"rb3{s}")
        ev_bc = bld.emit("pe", lambda e: e.matmul(ps_a[:, 1024:1536], ones_bf[0:1, 0:128],
                                                  rowb[0:1, 512:1024], start=False, stop=True),
                         inc=sE, tag=f"rb4{s}")
        ev_app = None
        for c in range(NCH):
            bld.emit("dve", (lambda c_, s_: (lambda e: e.tensor_mul(
                st_ap(c_, s_ * SW, SW), st_ap(c_, s_ * SW, SW), ps_a[:, 512:1024])))(c, s),
                waits=[ev_bc], tag=f"ap1{s}_{c}")
            ev_app = bld.emit("dve", (lambda c_, s_: (lambda e: e.tensor_add(
                st_ap(c_, s_ * SW, SW), st_ap(c_, s_ * SW, SW), ps_a[:, 1024:1536])))(c, s),
                inc=sV, tag=f"ap2{s}_{c}")
        r_app.set(s, ev_app)
        bld.dma("sp", sS, AP(x1d, s * SW, [[T, 128], [128 * T, 8], [1, SW]]),
                AP(stream, 1 + s * SW, [[8 * CST, 128], [CST, 8], [1, SW]]),
                waits=[ev_app], tag=f"spill{s}")
    bld.barrier(SEMS, tag="a3end")
    if _os0.environ.get("K_D_NOLN", "0") == "4":
        ev_o = bld.dma("sp", sS, AP(out_fm, 0, [[T, 128], [1, T]]),
                       AP(x1d, 0, [[T, 128], [1, T]]), tag="x1dump")
        for c in range(1, 8):
            ev_o = bld.dma("sp", sS, AP(out_fm, c * 128 * T, [[T, 128], [1, T]]),
                           AP(x1d, c * 128 * T, [[T, 128], [1, T]]), tag=f"x1dump{c}")
        bld.wait_only("sp", [ev_o], tag="final")
        return finish(nc)

    def finish_dump_stream():
        ev_o = bld.dma("sp", sS, AP(out_fm, 0, [[T, 128], [128 * T, 8], [1, T]]),
                       AP(stream, 1, [[8 * CST, 128], [CST, 8], [1, T]]), tag="outS")
        bld.wait_only("sp", [ev_o], tag="final")
        return finish(nc)

    if stage == "A":
        return finish_dump_stream()

    ev_id2 = bld.dma("pool", sID, ident_sb[:], D["ident"].ap(), tag="ident2")
    ev_mk = bld.dma("pool", sMK, masks_sb[:],
                    AP(D["masks"], 0, [[128, 128], [128 * 128, 2], [1, 128]]), tag="masks")

    def emit_rows_recip(o_sum_ap, waits, tagp):
        bld.emit("dve", (lambda a_: (lambda e: e.tensor_copy(rowf[0:1, 0:512], a_)))(o_sum_ap),
                 waits=waits, tag=tagp + "r1")
        bld.emit("dve", lambda e: e.reciprocal(rowf[32:33, 0:512], rowf[0:1, 0:512]), tag=tagp + "r2")
        bld.emit("dve", lambda e: e.tensor_copy(rowb[0:1, 0:512], rowf[32:33, 0:512]), tag=tagp + "r3")
        bld.emit("dve", lambda e: e.tensor_copy(rowf[64:65, 0:512], rowb[0:1, 0:512]), tag=tagp + "r4")
        bld.emit("dve", lambda e: e.tensor_copy(rowf[64:65, 512:1024], rowf[32:33, 0:512]), tag=tagp + "r5")
        return bld.emit("dve", lambda e: e.tensor_sub(rowb[0:1, 512:1024], rowf[64:65, 512:1024],
                                                      rowf[64:65, 0:512]), inc=sV, tag=tagp + "r6")

    def emit_bcast_psd(ev_rows, extra_waits, tagp):
        bld.emit("pe", lambda e: e.matmul(ps_d[0:64, :], ones_bf[0:1, 0:64], rowb[0:1, 0:512],
                                          start=True, stop=False), waits=[ev_rows] + extra_waits,
                 tag=tagp + "b1")
        evb = bld.emit("pe", lambda e: e.matmul(ps_d[0:64, :], ones_bf[0:1, 0:64], rowb[0:1, 512:1024],
                                                start=False, stop=True), inc=sE, tag=tagp + "b2")
        return bld.emit("dve", lambda e: e.tensor_copy(rowf[0:64, 0:512], ps_d[0:64, :]),
                        waits=[evb], inc=sV, tag=tagp + "be")

    r0 = rowf[0:1, 0:512]; r32 = rowf[32:33, 0:512]
    r64 = rowf[64:65, 0:512]; r64b = rowf[64:65, 512:1024]

    def emit_ln_rows(sum_ap, sumsq_ap, waits, tagp, r1_ap=None, r2_ap=None):
        if r1_ap is None:
            r1_ap = ps_a[:, 512:1024]
        if r2_ap is None:
            r2_ap = ps_a[:, 1024:1536]
        bld.emit("dve", (lambda a_: (lambda e: e.tensor_scalar(
            r0, a_, CC.col(1.0 / 1024.0, 0, 1), None, ALU.mult)))(sum_ap), waits=waits, tag=tagp + "mu")
        bld.emit("dve", (lambda a_: (lambda e: e.tensor_scalar(
            r32, a_, CC.col(1.0 / 1024.0, 0, 1), None, ALU.mult)))(sumsq_ap), tag=tagp + "e2")
        bld.emit("dve", lambda e: e.tensor_mul(r64, r0, r0), tag=tagp + "mu2")
        bld.emit("dve", lambda e: e.tensor_copy(r64b, r32), tag=tagp + "e2c")
        ev_var = bld.emit("dve", lambda e: e.tensor_sub(r32, r64b, r64), inc=sV, tag=tagp + "var")
        emit_ln_rows.last_var = ev_var
        bld.emit("act", lambda e: e.activation(r64, r32, AF.Ln, bias=CC.col(1e-5, 32, 1),
                                               scale=CC.col(1.0, 32, 1)), waits=[ev_var], tag=tagp + "ln")
        ev_rstd = bld.emit("act", lambda e: e.activation(r32, r64, AF.Exp, bias=CC.col(0.0, 64, 1),
                                                         scale=CC.col(-0.5, 64, 1)), inc=sA,
                           tag=tagp + "rstd")
        bld.emit("dve", lambda e: e.tensor_copy(r64, r32), waits=[ev_rstd], tag=tagp + "rsc")
        bld.emit("dve", lambda e: e.tensor_copy(r64b, r0), tag=tagp + "muc")
        bld.emit("dve", lambda e: e.tensor_mul(r0, r64, r64b), tag=tagp + "r2a")
        bld.emit("dve", lambda e: e.tensor_scalar(r0, r0, CC.col(-1.0, 0, 1), None, ALU.mult),
                 tag=tagp + "r2b")
        bld.emit("dve", lambda e: e.tensor_copy(rowb[0:1, 0:512], r32), tag=tagp + "h1")
        bld.emit("dve", lambda e: e.tensor_copy(r64b, rowb[0:1, 0:512]), tag=tagp + "h2")
        ev_hl = bld.emit("dve", lambda e: e.tensor_sub(rowb[0:1, 512:1024], r64, r64b),
                         inc=sV, tag=tagp + "h3")
        bld.emit("pe", lambda e: e.matmul(r1_ap, ones_bf[0:1, 0:128], rowb[0:1, 0:512],
                                          start=True, stop=False), waits=[ev_hl], tag=tagp + "p1")
        ev_rb2 = bld.emit("pe", lambda e: e.matmul(r1_ap, ones_bf[0:1, 0:128],
                                                   rowb[0:1, 512:1024], start=False, stop=True),
                          inc=sE, tag=tagp + "p2")
        bld.emit("dve", lambda e: e.tensor_copy(rowb[0:1, 0:512], r0), waits=[ev_rb2], tag=tagp + "h4")
        bld.emit("dve", lambda e: e.tensor_copy(r64, rowb[0:1, 0:512]), tag=tagp + "h5")
        bld.emit("dve", lambda e: e.tensor_copy(r64b, r0), tag=tagp + "h5b")
        ev_hl2 = bld.emit("dve", lambda e: e.tensor_sub(rowb[0:1, 512:1024], r64b, r64),
                          inc=sV, tag=tagp + "h6")
        bld.emit("pe", lambda e: e.matmul(r2_ap, ones_bf[0:1, 0:128], rowb[0:1, 0:512],
                                          start=True, stop=False), waits=[ev_hl2], tag=tagp + "p3")
        return bld.emit("pe", lambda e: e.matmul(r2_ap, ones_bf[0:1, 0:128],
                                                 rowb[0:1, 512:1024], start=False, stop=True),
                        inc=sE, tag=tagp + "p4")

    LN8 = float(np.log(8.0))
    _dmerge = (stage == "full") and _os0.environ.get("K_DM_OFF", "0") != "1"
    r_x1 = Ring()
    for lyr in range(2):
        tg = f"L{lyr}"
        ev_wqkv = bld.dma("pool", sWQ, wqkv_sb[:],
                          AP(D["wqkv"], lyr * 1024 * 1536, [[1536, 128], [128 * 1536, 8], [1, 1536]]),
                          tag=tg + "wqkv")
        ev_wo = bld.dma("pool", sWO, wo_sb[:],
                        AP(D["wo"], lyr * 512 * 1024, [[1024, 128], [128 * 1024, 4], [1, 1024]]),
                        tag=tg + "wo")
        ev_qs = bld.dma("sp", sQS, qsks[:, 0:64], AP(D["qs_t"], lyr * 128 * 64, [[64, 128], [1, 64]]),
                        tag=tg + "qs")
        ev_ks = bld.dma("sp", sKS, qsks[:, 64:128], AP(D["ks_t"], lyr * 128 * 64, [[64, 128], [1, 64]]),
                        tag=tg + "ks")
        ev_lc = []
        for (name, col0, w_) in (("lng_col", 0, 8), ("lnb_col", 8, 8), ("b1a_col", 16, 22),
                                 ("b1g_col", 38, 22), ("b2_col", 60, 8)):
            ev_lc.append(bld.dma("sp", sLC[len(ev_lc)], lcols[:, col0:col0 + w_],
                                 AP(D[name], lyr * 128 * w_, [[w_, 128], [1, w_]]), tag=tg + name))
        ev_vlo = bld.emit("dve", lambda e: e.memset(AP(vloc, 64, [[4160, 128], [65, 64], [1, 1]]), 1.0),
                          inc=sV, tag=tg + "vlones")
        ev_bnd = bld.emit("dve", lambda e: e.tensor_copy(
            AP(bound, 0, [[64, 128], [7, 8], [1, 7]]),
            AP(stream, 1 + 511, [[8 * CST, 128], [CST, 8], [512, 7]])), inc=sV, tag=tg + "bnd")

        _BNB = int(_os0.environ.get("K_B_NB", "8"))
        _BQONLY = _os0.environ.get("K_B_QKV_ONLY", "0") == "1"
        _BNOWO = _os0.environ.get("K_B_NOWO", "0") == "1"
        r_hb = Ring(); r_qkv = Ring(); r_vev = Ring(); r_tr = Ring(); r_qkev = Ring()
        r_expL = Ring(); r_pvL = Ring(); r_nmL = Ring(); r_stL = Ring(); r_wo = Ring(); r_rot = Ring()
        r_bceL = Ring(); r_d1L = Ring(); r_qch = Ring()
        pendL = [None]
        def b_flush_norm():
            if pendL[0] is None:
                return
            b_, h_, ev_pv_, oa = pendL[0]
            pendL[0] = None
            idx = b_ * 8 + h_
            ev_cp1 = bld.emit("dve", (lambda oa_: (lambda e: e.tensor_copy(
                rowf[0:1, 0:512], oa_[64:65, :])))(oa),
                waits=[ev_pv_], inc=sV, tag=f"{tg}nr0_{b_}_{h_}")
            bld.emit("act", lambda e: e.activation(
                rowf[96:97, 0:512], rowf[0:1, 0:512], AF.Ln,
                bias=CC.col(0.0, 0, 1), scale=CC.col(1.0, 0, 1)),
                waits=[ev_cp1], tag=f"{tg}nr1_{b_}_{h_}")
            ev_rec = bld.emit("act", lambda e: e.activation(
                rowf[32:33, 0:512], rowf[96:97, 0:512], AF.Exp,
                bias=CC.col(0.0, 96, 1), scale=CC.col(-1.0, 96, 1)),
                waits=[r_d1L.get(idx - 1)], inc=sA, tag=f"{tg}nr2_{b_}_{h_}")
            sl = idx % 8
            ev_d1 = bld.dma("sp", sN1, AP(x1d, 1024 * T + sl * 512, [[512, 1], [1, 512]]), rowf[32:33, 0:512],
                            waits=[ev_rec], tag=f"{tg}nrd1_{b_}_{h_}")
            r_d1L.set(idx, ev_d1)
            ev_d2 = bld.dma("sp", sN2, rowf[0:64, 512:1024], AP(x1d, 1024 * T + sl * 512, [[0, 64], [1, 512]]),
                            waits=[ev_d1, r_bceL.get(idx - 1)], tag=f"{tg}nrd2_{b_}_{h_}")
            hb_ = 64 * (h_ % 2)
            dst2 = oloc[hb_:hb_ + 64, (h_ // 2) * 512:(h_ // 2 + 1) * 512]
            ev_nm = bld.emit("dve", (lambda d3, oa_: (lambda e: e.tensor_mul(
                d3, oa_[0:64, :], rowf[0:64, 512:1024])))(dst2, oa),
                waits=[ev_d2], inc=sV, tag=f"{tg}nm{b_}_{h_}")
            r_bceL.set(idx, ev_nm)
            r_nmL.set((b_, h_), ev_nm)
        for b in range(_BNB):
            t0b = b * 512
            bpar = b % 2
            for ti in range(4):
                t0 = t0b + ti * 128
                rsl = ti % 2
                ev_rt = []
                for tbl, nm in enumerate(("rot_csq", "rot_ssq", "rot_csk", "rot_ssk")):
                    ev_rt.append(bld.dma("sp", sR[tbl * 2 + rsl],
                                         rotb[:, rsl * 256 + tbl * 64: rsl * 256 + tbl * 64 + 64],
                                         AP(D[nm], t0 * 64, [[64, 128], [1, 64]]),
                                         waits=[r_rot.get(b * 4 + ti - 2)], tag=f"{tg}rot{b}_{ti}_{tbl}"))
                ev_rot = ev_rt
                bld.emit("dve", (lambda t_: (lambda e: e.tensor_copy(
                    hbuf[:, 0:512].rearrange("p (c t) -> p c t", t=128),
                    AP(stream, 1 + t_, [[8 * CST, 128], [CST, 4], [1, 128]]))))(t0),
                    waits=[r_qkv.get((b, ti - 1)), r_hb.get((b, ti - 1)), ev_bnd],
                    tag=f"{tg}hbA{b}_{ti}")
                bld.emit("dve", (lambda t_: (lambda e: e.tensor_copy(
                    hbuf[:, 512:1024].rearrange("p (c t) -> p c t", t=128),
                    AP(stream, 1 + t_ - 1 + 4 * CST, [[8 * CST, 128], [CST, 4], [1, 128]]))))(t0),
                    tag=f"{tg}hbB{b}_{ti}")
                if ti == 0 and b > 0:
                    bld.emit("dve", (lambda b_: (lambda e: e.tensor_copy(
                        AP(hbuf, 512, [[1024, 128], [128, 4], [1, 1]]),
                        AP(bound, 4 * 7 + b_ - 1, [[64, 128], [7, 4], [1, 1]]))))(b),
                        tag=f"{tg}hbb{b}")
                ev_hb = bld.emit("dve", lambda e: e.engine_nop(), inc=sV, tag=f"{tg}hbf{b}_{ti}")
                r_hb.set((b, ti), ev_hb)
                ev_q = None
                for n in range(3):
                    for k in range(NCH):
                        ev_q = bld.emit("pe", (lambda n_, k_: (lambda e: e.matmul(
                            ps_a[:, n_ * 512:(n_ + 1) * 512], hbuf[:, k_ * 128:(k_ + 1) * 128],
                            wqkv_sb[:, k_ * 1536 + n_ * 512: k_ * 1536 + n_ * 512 + 512],
                            start=(k_ == 0), stop=(k_ == NCH - 1))))(n, k),
                            waits=[ev_hb, ev_wqkv, r_vev.get((b, ti - 1)), r_qkev.get((b, ti - 1)),
                                   r_nmL.get((b - 1, 7))],
                            inc=(sE if (n == 2 and k == NCH - 1) else None), tag=f"{tg}qkv{b}_{ti}_{n}_{k}")
                r_qkv.set((b, ti), ev_q)
                ev_v = bld.emit("dve", (lambda bp_, ti_: (lambda e: e.tensor_copy(
                    AP(vloc, bp_ * 2080 + ti_ * 520, [[4160, 128], [65, 8], [1, 64]]),
                    ps_a[:, 1024:1536].rearrange("p (h d) -> p h d", d=64))))(bpar, ti),
                    waits=[ev_q], inc=sV, tag=f"{tg}vev{b}_{ti}")
                r_vev.set((b, ti), ev_v)
                ev_tq = []
                seq0 = (b * 4 + ti) * 2
                _qkset = () if _os0.environ.get("K_B_NOQK", "0") == "1" else (
                    ("q", 0, 0, 0, LN8), ("k", 1, 16, 64, 0.0))
                for wi, (which, psl, ssoff, rcol, lnb_) in enumerate(_qkset):
                    _UPTO = int(_os0.environ.get("K_B_QK_UPTO", "11"))
                    pa = ps_a[:, psl * 512:(psl + 1) * 512]
                    bld.emit("act", (lambda pa_: (lambda e: e.activation(
                        q32[:], pa_, AF.Copy, bias=0.0, scale=1.0)))(pa),
                        waits=[ev_q, r_qch.get(seq0 + wi - 1)], tag=f"{tg}{which}c{b}_{ti}")
                    ev_sq = bld.emit("act", lambda e: e.activation(
                        qt1[:], q32[:], AF.Square, bias=CC.col(0.0), scale=CC.col(1.0)),
                        inc=sA, tag=f"{tg}{which}sq{b}_{ti}")
                    ev_ss = bld.emit("dve", (lambda so_: (lambda e: e.reduce_sum(
                        l2buf[:, so_:so_ + 8], qt1.rearrange("p (h d) -> p h d", d=64),
                        axis=AX.X)))(ssoff), waits=[ev_sq], inc=sV, tag=f"{tg}{which}rs{b}_{ti}")
                    bld.emit("pe", lambda e: e.matmul(
                        ps_d[0:64, 0:64], ones_bf[:, 0:64], ones_bf[:, 0:64],
                        start=True, stop=True), waits=[ev_ss], tag=f"{tg}{which}hb{b}_{ti}")
                    if _UPTO >= 5:
                        bld.emit("act", (lambda so_: (lambda e: e.activation(
                            l2buf[:, so_ + 8:so_ + 16], l2buf[:, so_:so_ + 8], AF.Ln,
                            bias=CC.col(1e-24), scale=CC.col(1.0))))(ssoff),
                            waits=[ev_ss], tag=f"{tg}{which}ln{b}_{ti}")
                    if _UPTO >= 6:
                        ev_r = bld.emit("act", (lambda so_, lb_: (lambda e: e.activation(
                            l2buf[:, so_:so_ + 8], l2buf[:, so_ + 8:so_ + 16], AF.Exp,
                            bias=CC.col(lb_), scale=CC.col(-0.5))))(ssoff, lnb_),
                            inc=sA, tag=f"{tg}{which}ex{b}_{ti}")
                    else:
                        ev_r = ev_ss
                    if _UPTO >= 7:
                        bld.emit("dve", (lambda rc_: (lambda e: e.tensor_tensor(
                            qt1.rearrange("p (h d) -> p h d", d=64),
                            q32.rearrange("p (h d) -> p h d", d=64),
                            AP(qsks, rc_, [[128, 128], [0, 8], [1, 64]]), ALU.mult)))(rcol),
                            waits=[ev_qs, ev_ks], tag=f"{tg}{which}s{b}_{ti}")
                    cs0 = rsl * 256 + (0 if which == "q" else 128)
                    if _UPTO >= 8:
                        bld.emit("dve", (lambda c0_: (lambda e: e.tensor_tensor(
                            qt2.rearrange("p (h d) -> p h d", d=64),
                            qt1.rearrange("p (h d) -> p h d", d=64),
                            AP(rotb, c0_, [[512, 128], [0, 8], [1, 64]]), ALU.mult)))(cs0),
                            waits=ev_rot, tag=f"{tg}{which}ca{b}_{ti}")
                    if _UPTO >= 9:
                        bld.emit("dve", (lambda c0_: (lambda e: e.tensor_tensor(
                            q32.rearrange("p (h s d) -> p h s d", s=2, d=32),
                            AP(qt1, 32, [[512, 128], [64, 8], [-32, 2], [1, 32]]),
                            AP(rotb, c0_ + 64, [[512, 128], [0, 8], [32, 2], [1, 32]]), ALU.mult)))(cs0),
                            tag=f"{tg}{which}cb{b}_{ti}")
                    if _UPTO >= 10:
                        bld.emit("dve", lambda e: e.tensor_add(qt1[:], qt2[:], q32[:]), tag=f"{tg}{which}ad{b}_{ti}")
                    ev_sc = bld.emit("dve", (lambda so_: (lambda e: e.tensor_tensor(
                        qbf.rearrange("p (h d) -> p h d", d=64),
                        qt1.rearrange("p (h d) -> p h d", d=64),
                        AP(l2buf, so_, [[32, 128], [1, 8], [0, 64]]), ALU.mult)))(ssoff),
                        waits=[ev_r], inc=sV, tag=f"{tg}{which}rr{b}_{ti}")
                    r_qch.set(seq0 + wi, ev_sc)
                    if _os0.environ.get("K_B_NOTR", "0") == "1":
                        ev_tq.append(ev_sc)
                        continue
                    tb = pt_t if which == "q" else ps_b[:].bitcast(BF16)
                    ev_t = None
                    for p in range(4):
                        ev_t = bld.emit("pe", (lambda p_, tb_: (lambda e: e.transpose(
                            tb_[:, p_ * 128:(p_ + 1) * 128],
                            qbf[:, p_ * 128:(p_ + 1) * 128], ident_sb[:])))(p, tb),
                            waits=[ev_sc, ev_id2, r_tr.get((b, ti - 1, which)),
                                   r_nmL.get((b - 1, 7))],
                            inc=(sE if p == 3 else None), tag=f"{tg}{which}tr{b}_{ti}_{p}")
                    if which == "q":
                        dst = AP(qfm, ti * 128, [[4 * 512, 128], [512, 4], [1, 128]])
                    else:
                        dst = AP(kfm, bpar * 512 + ti * 128, [[8 * 512, 128], [1024, 4], [1, 128]])
                    ev_te = bld.emit("dve", (lambda dst_, tb_: (lambda e: e.tensor_copy(
                        dst_, tb_[:, 0:512].rearrange("p (s c) -> p s c", c=128))))(dst, tb),
                        waits=[ev_t], inc=sV, tag=f"{tg}{which}te{b}_{ti}")
                    r_tr.set((b, ti, which), ev_te)
                    ev_tq.append(ev_te)
                ev_qk = bld.emit("dve", lambda e: e.engine_nop(), waits=ev_tq, inc=sV, tag=f"{tg}qkf{b}_{ti}")
                r_qkev.set((b, ti), ev_qk)
                r_rot.set(b * 4 + ti, ev_qk)
            bld.emit("dve", lambda e: e.memset(qfm_z[64:128, 0:512], 0.0), tag=f"{tg}zq0_{b}")
            ev_zq = bld.emit("dve", lambda e: e.memset(qfm_z[0:64, 512:1024], 0.0),
                             inc=sV, tag=f"{tg}zq1_{b}")
            for h in range(0 if _BQONLY else HL):
                hb = 64 * (h % 2)
                pr = h // 2
                oa_h = ps_d[:] if h % 2 == 0 else pt_t[:].bitcast(F32)[:, 0:512]
                worder = [4, 0, 1, 2, 3, 5, 6, 7] if b > 0 else [4, 5, 6, 7]
                pairs = [(4, 0), (1, 2), (3, 5), (6, 7)] if b > 0 else [(4, 5), (6, 7)]
                npair = len(pairs)
                ev_qz = bld.emit("dve", (lambda hb_, pr_, h_: (lambda e: e.tensor_copy(
                    qfm_z[hb_:hb_ + 64, (h_ % 2) * 512:(h_ % 2) * 512 + 512],
                    qfm[hb_:hb_ + 64, pr_ * 512: pr_ * 512 + 512])))(hb, pr, h),
                    waits=[ev_zq, r_stL.get((b, h - 2, npair - 1))],
                    inc=sV, tag=f"{tg}qz{b}_{h}")
                ev_pv = None
                for pi, pr2 in enumerate(pairs):
                    if pi == 1:
                        b_flush_norm()
                    rg = (pi % 2) * 1024
                    info = []
                    ev_st = None
                    for k2, w in enumerate(pr2):
                        side, j = w // 4, w % 4
                        if side == 0:
                            c0, c1 = 0, 128 * (j + 1)
                        else:
                            c0, c1 = 128 * j, 512
                        kb_par = (b - 1 + side) % 2
                        colof = rg + k2 * 512
                        bld.emit("pe", (lambda h_, pr_, kp_, j_, c0_, c1_, co_: (lambda e: e.matmul(
                            ps_a[:, co_ + c0_: co_ + c1_],
                            kfm[0:128, pr_ * 1024 + kp_ * 512 + j_ * 128: pr_ * 1024 + kp_ * 512 + j_ * 128 + 128],
                            qfm_z[0:128, (h_ % 2) * 512 + c0_: (h_ % 2) * 512 + c1_],
                            start=True, stop=False)))(h, pr, kb_par, j, c0, c1, colof),
                            waits=[ev_qz, r_qkev.get((b, 3)), r_vev.get((b, 3)),
                                   r_expL.get((b, h, pi - 2)),
                                   (r_expL.get((b, h - 1, pi - 2 + npair)) if pi < 2 else None)],
                            tag=f"{tg}st{b}_{h}_{w}")
                        ev_st = bld.emit("pe", (lambda j_, side_, co_: (lambda e: e.matmul(
                            ps_a[:, co_ + 128 * j_: co_ + 128 * j_ + 128],
                            masks_sb[:, side_ * 128:(side_ + 1) * 128], ident_sb[:, 0:128],
                            start=False, stop=True)))(j, side, colof),
                            waits=[ev_mk, ev_id2], inc=(sE if k2 == 1 else None),
                            tag=f"{tg}stm{b}_{h}_{w}")
                        info.append((side, j, c0, c1, kb_par, colof))
                    r_stL.set((b, h, pi), ev_st)
                    ev_ex = bld.emit("act", (lambda rg_, pi_: (lambda e: e.activation(
                        ploc[:, (pi_ % 2) * 1024: (pi_ % 2) * 1024 + 1024], ps_a[:, rg_:rg_ + 1024],
                        AF.Exp, bias=CC.col(-C_LOCAL), scale=CC.col(1.0))))(rg, pi),
                        waits=[ev_st, r_pvL.get((b, h, pi - 2)), ev_cc], inc=sA,
                        tag=f"{tg}ex{b}_{h}_{pi}")
                    r_expL.set((b, h, pi), ev_ex)
                    for k2, (side, j, c0, c1, kb_par, colof) in enumerate(info):
                        ev_pv = bld.emit("pe", (lambda h_, kp_, j_, c0_, c1_, co_, first_, last_, oa_: (lambda e: e.matmul(
                            oa_[0:65, c0_:c1_],
                            AP(vloc, kp_ * 2080 + j_ * 520 + h_ * 65, [[4160, 128], [1, 65]]),
                            ploc[:, co_ + c0_: co_ + c1_],
                            start=first_, stop=last_)))(h, kb_par, j, c0, c1, colof,
                                                        pi == 0 and k2 == 0,
                                                        pi == npair - 1 and k2 == 1, oa_h),
                            waits=[ev_ex, ev_vlo, (r_nmL.get((b, h - 2)) if (pi == 0 and k2 == 0) else None)],
                            inc=(sE if (pi == npair - 1 and k2 == 1) else None),
                            tag=f"{tg}pv{b}_{h}_{pi}_{k2}")
                    r_pvL.set((b, h, pi), ev_pv)
                pendL[0] = (b, h, ev_pv, oa_h)
            b_flush_norm()
            for m in range(0 if (_BQONLY or _BNOWO) else NCH):
                bank = ps_b if m % 2 == 0 else ps_c
                evp = None
                for k in range(4):
                    evp = bld.emit("pe", (lambda bank_, k_, m_: (lambda e: e.matmul(
                        bank_[:], wo_sb[:, k_ * 1024 + m_ * 128: k_ * 1024 + m_ * 128 + 128],
                        oloc[:, k_ * 512:(k_ + 1) * 512],
                        start=(k_ == 0), stop=(k_ == 3))))(bank, k, m),
                        waits=[ev_wo, r_nmL.get((b, 2 * k + 1)), r_wo.get((b, m - 2))],
                        inc=(sE if k == 3 else None), tag=f"{tg}wo{b}_{m}")
                ev_add = bld.emit("dve", (lambda bank_, m_, t_: (lambda e: e.tensor_add(
                    st_ap(m_, t_, 512), st_ap(m_, t_, 512), bank_[:])))(bank, m, t0b),
                    waits=[evp], inc=sV, tag=f"{tg}woa{b}_{m}")
                r_wo.set((b, m), ev_add)
            ev_blk = bld.emit("dve", lambda e: e.engine_nop(),
                              waits=[r_wo.get((b, m2)) for m2 in range(NCH)] +
                                    ([r_qkev.get((b, 3))] if _BQONLY else []) +
                                    ([r_nmL.get((b, 7))] if _BNOWO and not _BQONLY else []),
                              inc=sV, tag=f"{tg}blk{b}")
            r_qkv.set((b + 1, -1), ev_blk)
        bld.barrier(SEMS, tag=f"{tg}attn_end")
        if stage == "B0a" and lyr == 0:
            return finish_dump_stream()

        ev_bnd2 = bld.emit("dve", lambda e: e.tensor_copy(
            AP(bound, 0, [[64, 128], [7, 8], [1, 7]]),
            AP(stream, 1 + 511, [[8 * CST, 128], [CST, 8], [512, 7]])), inc=sV, tag=tg + "bnd2")
        r_w1 = Ring(); r_ag = Ring(); r_w2 = Ring(); r_sttM = Ring(); r_statM = Ring(); r_muM = Ring()
        r_w1m = Ring()
        r_appM = Ring()
        for sidx in range(NSTRIP):
            t0 = sidx * 512
            ev_h2 = None
            for c in range(NCH):
                if c < 4:
                    ev_h2 = bld.emit("dve", (lambda c_, t_: (lambda e: e.tensor_copy(
                        h2bf[:, c_ * 512:(c_ + 1) * 512], st_ap(c_, t_, 512))))(c, t0),
                        waits=[ev_bnd2, r_w1.get(sidx * NC2 - 1)], tag=f"{tg}h2{sidx}_{c}")
                elif sidx == 0:
                    ev_h2 = bld.emit("dve", (lambda c_, t_: (lambda e: e.tensor_copy(
                        h2bf[:, c_ * 512:(c_ + 1) * 512], st_ap(c_, t_ - 1, 512))))(c, t0),
                        tag=f"{tg}h2{sidx}_{c}")
                else:
                    bld.emit("dve", (lambda c_, s_: (lambda e: e.tensor_copy(
                        h2bf[:, c_ * 512: c_ * 512 + 1], bound[:, c_ * 7 + s_ - 1: c_ * 7 + s_])))(c, sidx),
                        tag=f"{tg}h2b{sidx}_{c}")
                    ev_h2 = bld.emit("dve", (lambda c_, t_: (lambda e: e.tensor_copy(
                        h2bf[:, c_ * 512 + 1:(c_ + 1) * 512], st_ap(c_, t_, 511))))(c, t0),
                        tag=f"{tg}h2{sidx}_{c}")
            ev_h2 = bld.emit("dve", lambda e: e.engine_nop(), inc=sV, tag=f"{tg}h2f{sidx}")
            evm2 = None
            for c in range(NCH):
                csl = c % 2
                ev_sq = bld.emit("act", (lambda c_, csl_: (lambda e: e.activation(
                    sqb2[:, csl_ * 512:(csl_ + 1) * 512], h2bf[:, c_ * 512:(c_ + 1) * 512],
                    AF.Square, bias=CC.col(0.0), scale=CC.col(1.0))))(c, csl),
                    waits=[ev_h2, r_statM.get((sidx, c - 2))], inc=sA, tag=f"{tg}sq{sidx}_{c}")
                bld.emit("pe", (lambda c_: (lambda e: e.matmul(
                    ps_d[0:1, :], ones_bf[:, 0:1], h2bf[:, c_ * 512:(c_ + 1) * 512],
                    start=(c_ == 0), stop=(c_ == 7))))(c),
                    waits=[ev_sq, r_muM.get(sidx - 1)], tag=f"{tg}s1{sidx}_{c}")
                evm2 = bld.emit("pe", (lambda c_, csl_: (lambda e: e.matmul(
                    ps_a[0:1, 0:512], ones_bf[:, 0:1], sqb2[:, csl_ * 512:(csl_ + 1) * 512],
                    start=(c_ == 0), stop=(c_ == 7))))(c, csl), inc=sE, tag=f"{tg}s2{sidx}_{c}")
                r_statM.set((sidx, c), evm2)
            ev_bcL = emit_ln_rows(ps_d[0:1, :], ps_a[0:1, 0:512], [evm2, r_appM.get(sidx - 1)],
                                  f"{tg}lr{sidx}")
            r_muM.set(sidx, emit_ln_rows.last_var)
            ev_ap = None
            for c in range(NCH):
                bld.emit("dve", (lambda c_: (lambda e: e.tensor_mul(
                    qt1[:], h2bf[:, c_ * 512:(c_ + 1) * 512], ps_a[:, 512:1024])))(c),
                    waits=[ev_bcL], tag=f"{tg}ap1{sidx}_{c}")
                bld.emit("dve", lambda e: e.tensor_add(qt1[:], qt1[:], ps_a[:, 1024:1536]),
                         tag=f"{tg}ap2{sidx}_{c}")
                ev_ap = bld.emit("dve", (lambda c_: (lambda e: e.tensor_scalar(
                    h2bf[:, c_ * 512:(c_ + 1) * 512], qt1[:], lcols[:, c_:c_ + 1],
                    lcols[:, 8 + c_:8 + c_ + 1], ALU.mult, ALU.add)))(c),
                    inc=sV, tag=f"{tg}ap3{sidx}_{c}")
            r_appM.set(sidx, ev_ap)
            for cc in range(NC2):
                wsl = cc % 2
                ev_w1h = []
                for hf in range(2):
                    ev_w1h.append(bld.dma("pool", (sW1a if hf == 0 else sW1b)[wsl],
                                  w1buf[:, wsl * 2048 + hf * 1024: wsl * 2048 + hf * 1024 + 1024],
                                  AP(wscr, W1_OFF + (lyr * NC1 + 2 * cc) * 8 * 128 * 128 + hf * 8 * 16384,
                                     [[128, 128], [16384, 8], [1, 128]]),
                                  waits=[r_w1m.get(sidx * NC2 + cc - 2), ev_pc],
                                  tag=f"{tg}w1{sidx}_{cc}_{hf}"))
                ags = []
                for hf, (ag, off) in enumerate((("a", 0), ("g", 8))):
                    sl = (0 if ag == "a" else 2) + cc % 2
                    evp = None
                    for k in range(NCH):
                        evp = bld.emit("pe", (lambda sl_, k_, off_, wsl_: (lambda e: e.matmul(
                            ps_a[:, sl_ * 512:(sl_ + 1) * 512],
                            w1buf[:, wsl_ * 2048 + (off_ + k_) * 128: wsl_ * 2048 + (off_ + k_) * 128 + 128],
                            h2bf[:, k_ * 512:(k_ + 1) * 512],
                            start=(k_ == 0), stop=(k_ == NCH - 1))))(sl, k, off, wsl),
                            waits=[ev_w1h[hf], ev_ap, r_ag.get((sidx, cc - 2, ag))],
                            inc=(sE if k == NCH - 1 else None), tag=f"{tg}m1{sidx}_{cc}{ag}")
                    ags.append(evp)
                r_w1m.set(sidx * NC2 + cc, ags[1])
                ev_a = bld.emit("dve", (lambda sl_, cc_: (lambda e: e.tensor_scalar(
                    a_st[:], ps_a[:, sl_ * 512:(sl_ + 1) * 512], lcols[:, 16 + cc_:16 + cc_ + 1],
                    None, ALU.add)))(0 + cc % 2, cc), waits=[ags[0]], tag=f"{tg}ae{sidx}_{cc}")
                ev_g = bld.emit("act", (lambda sl_, cc_: (lambda e: e.activation(
                    g_st[:], ps_a[:, sl_ * 512:(sl_ + 1) * 512], AF.Gelu,
                    bias=lcols[:, 38 + cc_:38 + cc_ + 1], scale=CC.col(1.0))))(2 + cc % 2, cc),
                    waits=[ags[1], r_w1.get(sidx * NC2 + cc - 1)], inc=sA, tag=f"{tg}ge{sidx}_{cc}")
                ev_act = bld.emit("dve", (lambda cc_: (lambda e: e.tensor_mul(
                    act_st[:, cc_ * 512:(cc_ + 1) * 512], a_st[:], g_st[:])))(cc),
                    waits=[ev_a, ev_g], inc=sV, tag=f"{tg}am{sidx}_{cc}")
                r_ag.set((sidx, cc, "a"), ev_act)
                r_ag.set((sidx, cc, "g"), ev_act)
                r_w1.set(sidx * NC2 + cc, ev_act)
            for m in range(NCH):
                wsl = m % 2
                ev_w2h = []
                for hf, (t0h, nth) in enumerate(((0, 11), (11, 11))):
                    ev_w2h.append(bld.dma("pool", (sW2a if hf == 0 else sW2b)[wsl],
                                  w2buf[:, wsl * 2816 + t0h * 128: wsl * 2816 + (t0h + nth) * 128],
                                  AP(wscr, W2_OFF + (lyr * 8 + m) * NC2 * 128 * 128 + t0h * 16384,
                                     [[128, 128], [16384, nth], [1, 128]]),
                                  waits=[r_w2.get(sidx * 8 + m - 2), ev_pc],
                                  tag=f"{tg}w2{sidx}_{m}_{hf}"))
                bank = ps_b if m % 2 == 0 else ps_c
                evp = None
                for c in range(NC2):
                    evp = bld.emit("pe", (lambda bank_, c_, wsl_: (lambda e: e.matmul(
                        bank_[:], w2buf[:, wsl_ * 2816 + c_ * 128: wsl_ * 2816 + c_ * 128 + 128],
                        act_st[:, c_ * 512:(c_ + 1) * 512],
                        start=(c_ == 0), stop=(c_ == NC2 - 1))))(bank, c, wsl),
                        waits=[ev_w2h[0 if c < 11 else 1], r_w1.get(sidx * NC2 + c),
                               r_sttM.get((sidx, m - 2))],
                        inc=(sE if c == NC2 - 1 else None), tag=f"{tg}m2{sidx}_{m}")
                ev_stt = bld.emit("dve", (lambda bank_, m_, t_: (lambda e: e.scalar_tensor_tensor(
                    st_ap(m_, t_, 512), bank_[:], lcols[:, 60 + m_:60 + m_ + 1],
                    st_ap(m_, t_, 512), ALU.add, ALU.add)))(bank, m, t0),
                    waits=[evp], inc=sV, tag=f"{tg}w2a{sidx}_{m}")
                r_sttM.set((sidx, m), ev_stt)
                r_w2.set(sidx * 8 + m, evp)
            if _dmerge and lyr == 1:
                for c in range(NCH):
                    xb = x0_buf if c % 2 == 0 else x1b
                    ev_x1 = bld.dma("sp", sD0 if c % 2 == 0 else sD1, xb[:],
                                    AP(x1d, c * 128 * T + t0, [[T, 128], [1, 512]]),
                                    waits=[r_x1.get((sidx * 8 + c - 2))], tag=f"dx1{sidx}_{c}")
                    import os as _osm
                    _wser = ([r_sttM.get((sidx, mm_)) for mm_ in range(NCH)]
                             if _osm.environ.get("K_DM_SER", "0") == "1" else [r_sttM.get((sidx, c))])
                    ev_s1 = bld.emit("dve", (lambda c_, t_, xb_: (lambda e: e.tensor_add(
                        st_ap(c_, t_, 512), st_ap(c_, t_, 512), xb_[:])))(c, t0, xb),
                        waits=[ev_x1] + _wser, inc=sV, tag=f"ds1{sidx}_{c}")
                    r_x1.set(sidx * 8 + c, ev_s1)
        if not (_dmerge and lyr == 1) or _os0.environ.get("K_DM_BAR", "0") == "1":
            bld.barrier(SEMS, tag=f"{tg}mlp_end")
        if stage == "B0" and lyr == 0:
            return finish_dump_stream()
    if stage == "B1":
        return finish_dump_stream()

    _DNOLN = int(_os0.environ.get("K_D_NOLN", "0"))
    if _DNOLN == 5:
        ev_o = None
        for c in range(8):
            ev_o = bld.dma("sp", sS, AP(out_fm, c * 128 * T, [[T, 128], [1, T]]),
                           AP(x1d, c * 128 * T, [[T, 128], [1, T]]), tag=f"x1dmp{c}")
        bld.wait_only("sp", [ev_o], tag="final")
        return finish(nc)
    r_statD = Ring(); r_muD = Ring(); r_appD = Ring()
    ev_outs = []
    for sidx in range(NSTRIP):
        t0 = sidx * 512
        ev_s1 = None
        for c in range(NCH):
            if _DNOLN == 2 or _dmerge:
                break
            xb = x0_buf if c % 2 == 0 else (qt1 if _os0.environ.get("K_DM_QT1", "0") == "1" else q32)
            ev_x1 = bld.dma("sp", sD0 if c % 2 == 0 else sD1, xb[:],
                            AP(x1d, c * 128 * T + t0, [[T, 128], [1, 512]]),
                            waits=[r_x1.get((sidx * 8 + c - 2))], tag=f"dx1{sidx}_{c}")
            if _DNOLN == 3:
                ev_s1 = bld.emit("dve", (lambda c_, t_, xb_: (lambda e: e.tensor_copy(
                    st_ap(c_, t_, 512), xb_[:])))(c, t0, xb),
                    waits=[ev_x1], inc=sV, tag=f"ds1{sidx}_{c}")
            else:
                ev_s1 = bld.emit("dve", (lambda c_, t_, xb_: (lambda e: e.tensor_add(
                    st_ap(c_, t_, 512), st_ap(c_, t_, 512), xb_[:])))(c, t0, xb),
                    waits=[ev_x1], inc=sV, tag=f"ds1{sidx}_{c}")
            r_x1.set((sidx * 8 + c), ev_s1)
        if _DNOLN:
            ev_o = bld.dma("sp", sS, AP(out_fm, t0, [[T, 128], [128 * T, 8], [1, 512]]),
                           AP(stream, 1 + t0, [[8 * CST, 128], [CST, 8], [1, 512]]),
                           waits=[ev_s1], tag=f"doutn{sidx}")
            ev_outs.append(ev_o)
            r_appD.set(sidx, ev_s1)
            continue
        evm2 = None
        for c in range(NCH):
            csl = c % 2
            bld.emit("act", (lambda c_, csl_, t_: (lambda e: e.activation(
                s1bf[:, csl_ * 512:(csl_ + 1) * 512], st_ap(c_, t_, 512),
                AF.Copy, bias=0.0, scale=1.0)))(c, csl, t0),
                waits=[r_x1.get(sidx * 8 + c), r_statD.get((sidx, c - 2)),
                       (r_x1.get(NSTRIP * 8 - 1) if _dmerge else None)], tag=f"dsb{sidx}_{c}")
            ev_sqd = bld.emit("act", (lambda c_, csl_, t_: (lambda e: e.activation(
                sqbf[:, csl_ * 512:(csl_ + 1) * 512], st_ap(c_, t_, 512),
                AF.Square, bias=CC.col(0.0), scale=CC.col(1.0))))(c, csl, t0),
                inc=sA, tag=f"dsq{sidx}_{c}")
            xw_ = ([r_w1.get(NSTRIP * NC2 - 1), r_appM.get(NSTRIP - 1),
                    r_muM.get(NSTRIP - 1)] if (_dmerge and sidx == 0 and c == 0) else [])
            bld.emit("pe", (lambda c_, csl_: (lambda e: e.matmul(
                ps_d[0:1, :], ones_bf[:, 0:1], s1bf[:, csl_ * 512:(csl_ + 1) * 512],
                start=(c_ == 0), stop=(c_ == 7))))(c, csl),
                waits=[ev_sqd, r_muD.get(sidx - 1)] + xw_, tag=f"dm1{sidx}_{c}")
            evm2 = bld.emit("pe", (lambda c_, csl_: (lambda e: e.matmul(
                ps_a[0:1, 0:512], ones_bf[:, 0:1], sqbf[:, csl_ * 512:(csl_ + 1) * 512],
                start=(c_ == 0), stop=(c_ == 7))))(c, csl), inc=sE, tag=f"dm2{sidx}_{c}")
            r_statD.set((sidx, c), evm2)
        if sidx % 2 == 0:
            r1d, r2d = ps_a[:, 512:1024], ps_a[:, 1024:1536]
        else:
            r1d, r2d = ps_b[:], ps_c[:]
        ev_bcD = emit_ln_rows(ps_d[0:1, :], ps_a[0:1, 0:512],
                              [evm2, r_appD.get(sidx - 2),
                               (r_sttM.get((NSTRIP - 1, NCH - 2)) if (_dmerge and sidx == 1) else None),
                               (r_sttM.get((NSTRIP - 1, NCH - 1)) if (_dmerge and sidx == 1) else None)],
                              f"dl{sidx}", r1_ap=r1d, r2_ap=r2d)
        r_muD.set(sidx, emit_ln_rows.last_var)
        ev_app = None
        for c in range(NCH):
            bld.emit("dve", (lambda c_, t_, r1_: (lambda e: e.tensor_mul(
                st_ap(c_, t_, 512), st_ap(c_, t_, 512), r1_)))(c, t0, r1d),
                waits=[ev_bcD], tag=f"dap1{sidx}_{c}")
            ev_app = bld.emit("dve", (lambda c_, t_, r2_: (lambda e: e.tensor_add(
                st_ap(c_, t_, 512), st_ap(c_, t_, 512), r2_)))(c, t0, r2d),
                inc=sV, tag=f"dap2{sidx}_{c}")
        r_appD.set(sidx, ev_app)
        ev_o = bld.dma("sp", sS, AP(out_fm, t0, [[T, 128], [128 * T, 8], [1, 512]]),
                       AP(stream, 1 + t0, [[8 * CST, 128], [CST, 8], [1, 512]]),
                       waits=[ev_app], tag=f"dout{sidx}")
        ev_outs.append(ev_o)
    bld.wait_only("sp", [ev_outs[-1]], tag="final")
    return finish(nc)



def kernel(**inputs):
    import os
    ncores = int(os.environ.get("K_NCORES", "8"))
    shared, xs = _host_prep(inputs)
    nc = build_program(STAGE)
    in_maps = [dict(shared, x_fm=xs[i]) for i in range(ncores)]
    res = run_bass_kernel_spmd(nc, in_maps, core_ids=list(range(ncores)))
    out = np.stack([np.asarray(r["out_fm"]).T for r in res.results], axis=0)
    if ncores < 8:
        out = np.concatenate([out] + [out[:1]] * (8 - ncores), axis=0)
    return np.ascontiguousarray(out)
